# version 09
# baseline (speedup 1.0000x reference)
import numpy as np
import ml_dtypes
from contextlib import ExitStack

import concourse.bass as bass
import concourse.mybir as mybir
from concourse.bass_utils import run_bass_kernel_spmd

F32 = mybir.dt.float32
BF16 = mybir.dt.bfloat16
AF = mybir.ActivationFunctionType
ALU = mybir.AluOpType
AX = mybir.AxisListType.X
BF = ml_dtypes.bfloat16

S = 8192
OWN = 2048
D = 1024
C = 16
R = 64
E = 8
H = 2048
SCALE = 0.03125
EPS = 1e-5
BIG = 50.0
NG = S // 512
OWN_G = OWN // 512
OWN_T = OWN // 128

DEBUG_STAGE = 4
A_NG = NG
STRIP_C = C
STRIP_T = 64
FFN_M = 2
FFN_EI = 9
FFN_EIS = None
RESID = True
FFN_W13 = True
FFN_W2 = True

ENGS = ("tensor", "vector", "scalar", "sync", "gpsimd")


class Sched:
    def __init__(self, sems):
        self.ops = {e: [] for e in ENGS}
        self.cnt = {s: 0 for s in sems}
        self.waited = {e: {} for e in ENGS}
        self.owner = {}
        self.all_waits = {}
        self.dma_emits = []

    def own(self, sem, eng):
        self.owner[sem] = eng

    def tok(self, sem):
        return (sem, self.cnt[sem])

    def emit(self, eng, fn=None, waits=(), inc=None, inc_k=1, hard_waits=()):
        wl = []
        for (s, v), hard in [(w, True) for w in hard_waits] + [(w, False) for w in waits]:
            if v <= 0:
                continue
            assert self.cnt[s] >= v, f"wait for {v} but only {self.cnt[s]} emitted"
            self.all_waits.setdefault(s, set()).add(v)
            if self.owner.get(s) == eng and not hard:
                continue
            if self.waited[eng].get(s, 0) >= v:
                continue
            self.waited[eng][s] = v
            wl.append((s, v))
        tok = None
        prewait = [None]
        if inc is not None:
            if inc_k % 16 == 0:
                self.dma_emits.append((eng, len(self.ops[eng]), inc, self.cnt[inc], prewait))
            self.cnt[inc] += inc_k
            tok = (inc, self.cnt[inc])

        def op(e, fn=fn, wl=tuple(wl), inc=inc, inc_k=inc_k, prewait=prewait):
            if prewait[0] is not None:
                e.wait_ge(*prewait[0])
            for (s, v) in wl:
                e.wait_ge(s, v)
            r = fn(e) if fn is not None else None
            if inc is not None and r != "self_inc":
                if r is None:
                    r = e.nop()
                r.then_inc(inc, inc_k)
        op.wl = wl
        self.ops[eng].append(op)
        return tok

    def finalize(self):
        sorted_waits = {s: sorted(vs) for s, vs in self.all_waits.items()}
        import bisect
        want = {}
        for (eng, idx, sem, cnt_before, prewait) in self.dma_emits:
            vs = sorted_waits.get(sem)
            if not vs:
                continue
            i = bisect.bisect_right(vs, cnt_before)
            if i == 0:
                continue
            want[(eng, idx)] = (sem, vs[i - 1], prewait)
        for eng in ENGS:
            cur = {}
            for idx, op in enumerate(self.ops[eng]):
                w = want.get((eng, idx))
                if w is not None:
                    sem, b, prewait = w
                    if cur.get(sem, 0) < b:
                        prewait[0] = (sem, b)
                        cur[sem] = b
                for (s, v) in op.wl:
                    cur[s] = max(cur.get(s, 0), v)

    def fence(self):
        return [self.tok(s) for s in self.cnt]


class Arena:
    def __init__(self, ap, total_cols):
        self.ap = ap
        self.total = total_cols
        self.off = 0

    def take(self, cols):
        cols = (cols + 15) & ~15
        assert self.off + cols <= self.total, f"arena overflow {self.off}+{cols}>{self.total}"
        sl = self.ap[:, self.off:self.off + cols]
        self.off += cols
        return sl

    def reset_to(self, off=0):
        self.off = off




def emit_ffn_phase(nc, sch, sems, ab, af, ps, env):
    s_t, s_v, s_s, s_ld, s_st, s_w = sems
    acc = env["acc"]; sel_sb = env["sel_sb"]; wT_sb = env["wT_sb"]
    xnT_hb = env["xnT_hb"]; const_tok = env["const_tok"]
    w1b = env["w1b"]; w2b = env["w2b"]; w3b = env["w3b"]
    ws1b = env["ws1b"]; ws2b = env["ws2b"]; ws3b = env["ws3b"]
    precast_tok = env["precast_tok"]
    fence_g = sch.fence()
    ab.reset_to(0)
    af.reset_to(0)
    xnT_own = ab.take(8 * OWN)
    h_sb = ab.take(16 * 1024)
    w13 = [ab.take(4096), ab.take(4096)]
    w2sl = [ab.take(4096), ab.take(4096)]
    sl_sb = [ab.take(1024), ab.take(1024)]
    wbc = af.take(1024)
    t_sb = [af.take(512), af.take(512)]

    for dc in range(8):
        sch.emit("sync", lambda e, dc=dc: e.dma_start(
            out=xnT_own[:, dc * OWN:(dc + 1) * OWN], in_=xnT_hb[dc * 128:(dc + 1) * 128, :]),
            waits=fence_g, inc=s_ld, inc_k=16)
    xnT_tok = sch.tok(s_ld)

    ps1 = [ps[:, 0:1024], ps[:, 1024:2048]]
    ps3 = [ps[:, 2048:3072], ps[:, 3072:4096]]
    w13_free = [list(fence_g), list(fence_g)]
    w2_free = [list(fence_g), list(fence_g)]
    sl_free = [list(fence_g), list(fence_g)]
    comb_tok = {dc: list(fence_g) for dc in range(8)}
    h_tok = {}
    rnd = 0
    expert_fence = list(fence_g)

    for m in range(FFN_M):
        for ei in (FFN_EIS if FFN_EIS is not None else range(FFN_EI)):
            shared = (ei == 8)
            wbc_toks = []
            if not shared:
                for sg in range(2):
                    sgg = m * 2 + sg
                    bmm = sch.emit("tensor", lambda e, ei=ei, sgg=sgg, sg=sg: e.matmul(
                        ps[:, sg * 512:(sg + 1) * 512], sel_sb[:, ei * 128:(ei + 1) * 128],
                        wT_sb[:, sgg * 512:(sgg + 1) * 512],
                        start=True, stop=True),
                        waits=[const_tok] + expert_fence, inc=s_t)
                    wv = sch.emit("vector", lambda e, sg=sg: e.tensor_copy(
                        wbc[:, sg * 512:(sg + 1) * 512], ps[:, sg * 512:(sg + 1) * 512]),
                        waits=[bmm], inc=s_v)
                    wbc_toks.append(wv)
            first_rnd = rnd
            for hgg in (range(8) if FFN_W13 else ()):
                wsl = hgg % 2
                h0 = hgg * 256
                srcs = (ws1b, ws3b) if shared else (w1b[ei], w3b[ei])

                def ld13(e, wsl=wsl, h0=h0, srcs=srcs):
                    o3 = w13[wsl].rearrange("p (dc x) -> p dc x", dc=8)
                    e.dma_start(out=o3[:, :, 0:256],
                                in_=srcs[0].rearrange("(dc p) h -> p dc h", p=128)[:, :, h0:h0 + 256]).then_inc(s_ld, 16)
                    e.dma_start(out=o3[:, :, 256:512],
                                in_=srcs[1].rearrange("(dc p) h -> p dc h", p=128)[:, :, h0:h0 + 256]).then_inc(s_ld, 16)
                    return "self_inc"
                ldt = sch.emit("sync", ld13, waits=w13_free[wsl] + [precast_tok], inc=s_ld, inc_k=16 * 2)
                for sg in range(2):
                    pp = rnd % 2
                    sgg = m * 2 + sg

                    def do_w13(e, wsl=wsl, sgg=sgg, pp=pp):
                        r = None
                        for hc in range(2):
                            for dc in range(8):
                                e.matmul(ps1[pp][:, hc * 512:(hc + 1) * 512],
                                         w13[wsl][:, dc * 512 + hc * 128: dc * 512 + (hc + 1) * 128],
                                         xnT_own[:, dc * OWN + sgg * 512: dc * OWN + (sgg + 1) * 512],
                                         start=(dc == 0), stop=(dc == 7))
                        for hc in range(2):
                            for dc in range(8):
                                r = e.matmul(ps3[pp][:, hc * 512:(hc + 1) * 512],
                                             w13[wsl][:, dc * 512 + 256 + hc * 128: dc * 512 + 256 + (hc + 1) * 128],
                                             xnT_own[:, dc * OWN + sgg * 512: dc * OWN + (sgg + 1) * 512],
                                             start=(dc == 0), stop=(dc == 7))
                        return r
                    mm_w = [ldt, xnT_tok] + h_tok.get(rnd - 2, [])
                    if rnd - first_rnd < 2:
                        mm_w = mm_w + expert_fence + wbc_toks
                    mm_tok = sch.emit("tensor", do_w13, waits=mm_w, inc=s_t)
                    si_tok = sch.emit("scalar", lambda e, pp=pp, wsl=wsl: e.activation(
                        sl_sb[wsl], ps3[pp], AF.Silu),
                        waits=[mm_tok] + sl_free[wsl], inc=s_s)
                    mu_tok = sch.emit("vector", lambda e, pp=pp, wsl=wsl, hgg=hgg, sg=sg: e.tensor_mul(
                        h_sb.rearrange("p (hc s) -> p hc s", hc=16)[:, hgg * 2:hgg * 2 + 2, sg * 512:(sg + 1) * 512],
                        ps1[pp].rearrange("p (hc s) -> p hc s", hc=2),
                        sl_sb[wsl].rearrange("p (hc s) -> p hc s", hc=2)),
                        waits=[si_tok], inc=s_v)
                    sl_free[wsl] = [mu_tok]
                    h_tok[rnd] = [mu_tok]
                    rnd += 1
                w13_free[wsl] = h_tok[rnd - 1]
            h_all = h_tok.get(rnd - 1, list(fence_g))
            mm2_dc = {}
            prev_cv2 = {0: (), 1: ()}
            for sg in (range(2) if FFN_W2 else ()):
                sgg = m * 2 + sg
                for hg in range(4):
                    wsl = hg % 2
                    w2src = ws2b if shared else w2b[ei]

                    def ldw2(e, wsl=wsl, hg=hg, w2src=w2src):
                        return e.dma_start(
                            out=w2sl[wsl].rearrange("p (hc d) -> p hc d", hc=4),
                            in_=w2src[hg * 512:(hg + 1) * 512, :].rearrange("(hc p) d -> p hc d", p=128))
                    w2t = sch.emit("sync", ldw2, waits=w2_free[wsl] + [precast_tok], inc=s_ld, inc_k=16)
                    for dc in range(8):
                        def do_w2(e, sg=sg, hg=hg, wsl=wsl, dc=dc):
                            r = None
                            for hc in range(4):
                                r = e.matmul(ps[:, dc * 512:(dc + 1) * 512],
                                             w2sl[wsl][:, hc * 1024 + dc * 128: hc * 1024 + (dc + 1) * 128],
                                             h_sb[:, (hg * 4 + hc) * 1024 + sg * 512: (hg * 4 + hc) * 1024 + (sg + 1) * 512],
                                             start=(hg == 0 and hc == 0), stop=(hg == 3 and hc == 3))
                            return r
                        ww = [w2t] + h_all
                        if hg == 0:
                            ww = ww + comb_tok[dc]
                        mm2_dc[dc] = sch.emit("tensor", do_w2, waits=ww, inc=s_t)
                    w2_free[wsl] = [mm2_dc[7]]
                for dc in range(8):
                    if shared:
                        cv = sch.emit("vector", lambda e, dc=dc, sgg=sgg: e.tensor_add(
                            acc[:, dc * OWN + sgg * 512: dc * OWN + (sgg + 1) * 512],
                            acc[:, dc * OWN + sgg * 512: dc * OWN + (sgg + 1) * 512],
                            ps[:, dc * 512:(dc + 1) * 512]),
                            waits=[mm2_dc[dc]], inc=s_v)
                    else:
                        mt = sch.emit("vector", lambda e, dc=dc, sg=sg: e.tensor_mul(
                            t_sb[dc % 2], ps[:, dc * 512:(dc + 1) * 512], wbc[:, sg * 512:(sg + 1) * 512]),
                            waits=[mm2_dc[dc]],
                            hard_waits=list(prev_cv2[dc % 2]), inc=s_v)
                        cv = sch.emit("vector", lambda e, dc=dc, sgg=sgg: e.tensor_add(
                            acc[:, dc * OWN + sgg * 512: dc * OWN + (sgg + 1) * 512],
                            acc[:, dc * OWN + sgg * 512: dc * OWN + (sgg + 1) * 512], t_sb[dc % 2]),
                            hard_waits=[mt], inc=s_v)
                        prev_cv2[dc % 2] = [cv]
                    comb_tok[dc] = [cv]
            expert_fence = comb_tok[7]



def build_nc(debug_stage=4):
    nc = bass.Bass("TRN2", target_bir_lowering=False)

    xf = nc.dram_tensor("xf", [S, D], F32, kind="ExternalInput").ap()
    ut = nc.dram_tensor("ut", [D, D], F32, kind="ExternalInput").ap()
    vt = nc.dram_tensor("vt", [D, D], F32, kind="ExternalInput").ap()
    u_in = nc.dram_tensor("u_in", [D, D], F32, kind="ExternalInput").ap()
    v_in = nc.dram_tensor("v_in", [D, D], F32, kind="ExternalInput").ap()
    wr = nc.dram_tensor("wr", [D, E], F32, kind="ExternalInput").ap()
    rb_rep = nc.dram_tensor("rb_rep", [128, E], F32, kind="ExternalInput").ap()
    w1 = nc.dram_tensor("w1", [E, D, H], F32, kind="ExternalInput").ap()
    w2 = nc.dram_tensor("w2", [E, H, D], F32, kind="ExternalInput").ap()
    w3 = nc.dram_tensor("w3", [E, D, H], F32, kind="ExternalInput").ap()
    ws1 = nc.dram_tensor("ws1", [D, H], F32, kind="ExternalInput").ap()
    ws2 = nc.dram_tensor("ws2", [H, D], F32, kind="ExternalInput").ap()
    ws3 = nc.dram_tensor("ws3", [D, H], F32, kind="ExternalInput").ap()

    out = nc.dram_tensor("out", [OWN, D], F32, kind="ExternalOutput").ap()

    zu_fm = nc.dram_tensor("zu_fm", [D, S], BF16, kind="Internal").ap()
    zv_fm = nc.dram_tensor("zv_fm", [D, S], BF16, kind="Internal").ap()
    zu_tm = nc.dram_tensor("zu_tm", [S, D], BF16, kind="Internal").ap()
    zv_tm = nc.dram_tensor("zv_tm", [S, D], BF16, kind="Internal").ap()
    xnT_hb = nc.dram_tensor("xnT_hb", [D, OWN], BF16, kind="Internal").ap()
    w1b = nc.dram_tensor("w1b", [E, D, H], BF16, kind="Internal").ap()
    w2b = nc.dram_tensor("w2b", [E, H, D], BF16, kind="Internal").ap()
    w3b = nc.dram_tensor("w3b", [E, D, H], BF16, kind="Internal").ap()
    ws1b = nc.dram_tensor("ws1b", [D, H], BF16, kind="Internal").ap()
    ws2b = nc.dram_tensor("ws2b", [H, D], BF16, kind="Internal").ap()
    ws3b = nc.dram_tensor("ws3b", [D, H], BF16, kind="Internal").ap()
    ub = nc.dram_tensor("ub", [D, D], BF16, kind="Internal").ap()
    vb = nc.dram_tensor("vb", [D, D], BF16, kind="Internal").ap()

    identf = nc.inline_tensor(np.eye(128, dtype=np.float32), "identf").ap()
    identb = nc.inline_tensor(np.eye(128, dtype=BF), "identb").ap()
    sel_np = np.zeros((8, 8 * 128), dtype=BF)
    for e_ in range(8):
        sel_np[e_, e_ * 128:(e_ + 1) * 128] = 1.0
    sel_c = nc.inline_tensor(sel_np, "sel_c").ap()

    tb = {}
    if debug_stage == 5:
        tb["xnT_in"] = nc.dram_tensor("xnT_in", [D, OWN], BF16, kind="ExternalInput").ap()
        tb["wT_in"] = nc.dram_tensor("wT_in", [8, OWN], BF16, kind="ExternalInput").ap()
        tb["acc_out"] = nc.dram_tensor("acc_out", [128, 8 * OWN], F32, kind="ExternalOutput").ap()
    dbg = {}
    if debug_stage == 0:
        dbg["xn"] = nc.dram_tensor("d_xn", [S, D], F32, kind="ExternalOutput").ap()
        dbg["stats"] = nc.dram_tensor("d_stats", [S, 8], F32, kind="ExternalOutput").ap()
    if debug_stage < 4:
        dbg["zu_fm"] = nc.dram_tensor("d_zu_fm", [D, S], BF16, kind="ExternalOutput").ap()
        dbg["zv_fm"] = nc.dram_tensor("d_zv_fm", [D, S], BF16, kind="ExternalOutput").ap()
        dbg["zu_tm"] = nc.dram_tensor("d_zu_tm", [S, D], BF16, kind="ExternalOutput").ap()
        dbg["xnT"] = nc.dram_tensor("d_xnT", [D, OWN], BF16, kind="ExternalOutput").ap()
        dbg["wfull"] = nc.dram_tensor("d_wfull", [128, 128], F32, kind="ExternalOutput").ap()
        if debug_stage >= 2:
            dbg["qh1"] = nc.dram_tensor("d_qh1", [128, 8 * OWN], BF16, kind="ExternalOutput").ap()
            dbg["qh2"] = nc.dram_tensor("d_qh2", [128, 8 * OWN], BF16, kind="ExternalOutput").ap()
        if debug_stage >= 3:
            dbg["acc"] = nc.dram_tensor("d_acc", [128, 8 * OWN], F32, kind="ExternalOutput").ap()

    AB_COLS = 55296
    AF_COLS = 7168

    with ExitStack() as ctx:
        en = ctx.enter_context
        s_t = en(nc.semaphore("s_t"))
        s_v = en(nc.semaphore("s_v"))
        s_s = en(nc.semaphore("s_s"))
        s_ld = en(nc.semaphore("s_ld"))
        s_st = en(nc.semaphore("s_st"))
        s_w = en(nc.semaphore("s_w"))
        sch = Sched([s_t, s_v, s_s, s_ld, s_st, s_w])
        sch.own(s_t, "tensor"); sch.own(s_v, "vector"); sch.own(s_s, "scalar")

        acc = en(nc.sbuf_tensor("acc", [128, 8 * OWN], F32))[:, :]
        idf_sb = en(nc.sbuf_tensor("idf_sb", [128, 128], F32))[:, :]
        idb_sb = en(nc.sbuf_tensor("idb_sb", [128, 128], BF16))[:, :]
        sel_sb = en(nc.sbuf_tensor("sel_sb", [8, 8 * 128], BF16))[:, :]
        rb_sb = en(nc.sbuf_tensor("rb_sb", [128, E], F32))[:, :]
        wfull_sb = en(nc.sbuf_tensor("wfull_sb", [128, 128], F32))[:, :]
        wT_sb = en(nc.sbuf_tensor("wT_sb", [8, OWN], BF16))[:, :]
        stats = en(nc.sbuf_tensor("stats", [128, 24], F32))[:, :]
        rtr = en(nc.sbuf_tensor("rtr", [128, 48], F32))[:, :]
        wr_sb = en(nc.sbuf_tensor("wr_sb", [128, 8 * E], BF16))[:, :]

        negbig = en(nc.sbuf_tensor("negbig", [128, E], F32))[:, :]
        rmask = en(nc.sbuf_tensor("rmask", [128, E], mybir.dt.uint32))[:, :]
        ab = Arena(en(nc.sbuf_tensor("arena_b", [128, AB_COLS], BF16))[:, :], AB_COLS)
        af = Arena(en(nc.sbuf_tensor("arena_f", [128, AF_COLS], F32))[:, :], AF_COLS)

        ps = en(nc.psum_tensor("ps", [128, 4096], F32))[:, :]

        sch.emit("sync", lambda e: e.dma_start(out=idf_sb, in_=identf[:, :]), inc=s_ld, inc_k=16)
        sch.emit("sync", lambda e: e.dma_start(out=idb_sb, in_=identb[:, :]), inc=s_ld, inc_k=16)
        sch.emit("sync", lambda e: e.dma_start(out=sel_sb, in_=sel_c[:, :]), inc=s_ld, inc_k=16)
        sch.emit("sync", lambda e: e.dma_start(out=rb_sb, in_=rb_rep[:, :]), inc=s_ld, inc_k=16)
        const_tok = sch.tok(s_ld)
        sch.emit("vector", lambda e: e.memset(negbig, -BIG), inc=s_v)

        ut_sb = ab.take(8192)
        vt_sb = ab.take(8192)
        for dc in range(8):
            sch.emit("gpsimd", lambda e, dc=dc: e.dma_start(
                out=ut_sb[:, dc * 1024:(dc + 1) * 1024], in_=ut[dc * 128:(dc + 1) * 128, :]),
                inc=s_w, inc_k=16)
            sch.emit("gpsimd", lambda e, dc=dc: e.dma_start(
                out=vt_sb[:, dc * 1024:(dc + 1) * 1024], in_=vt[dc * 128:(dc + 1) * 128, :]),
                inc=s_w, inc_k=16)
        sch.emit("gpsimd", lambda e: e.dma_start(
            out=wr_sb, in_=wr.rearrange("(dc p) e -> p dc e", p=128)), inc=s_w, inc_k=16)
        wprep_tok = sch.tok(s_w)

        if debug_stage == 5:
            sch.emit("sync", lambda e: e.dma_start(
                out=xnT_hb[:, :], in_=tb["xnT_in"][:, :]), inc=s_st, inc_k=16)
            sch.emit("sync", lambda e: e.dma_start(out=wT_sb, in_=tb["wT_in"][:, :]),
                     inc=s_ld, inc_k=16)
            sch.emit("vector", lambda e: e.memset(acc, 0.0), inc=s_v)
            emit_ffn_phase(nc, sch, (s_t, s_v, s_s, s_ld, s_st, s_w), ab, af, ps, dict(
                acc=acc, sel_sb=sel_sb, wT_sb=wT_sb, xnT_hb=xnT_hb, const_tok=const_tok,
                w1b=w1b, w2b=w2b, w3b=w3b, ws1b=ws1b, ws2b=ws2b, ws3b=ws3b,
                precast_tok=precast_tok))
            fence = sch.fence()
            sch.emit("sync", lambda e: e.dma_start(out=tb["acc_out"][:, :], in_=acc),
                     waits=fence, inc=s_st, inc_k=16)
            sch.emit("sync", waits=[sch.tok(s_st), sch.tok(s_ld), sch.tok(s_w)])
            sch.finalize()
            with nc.Block() as block:
                block.tensor(lambda e: [op(e) for op in sch.ops["tensor"]])
                block.vector(lambda e: [op(e) for op in sch.ops["vector"]])
                block.scalar(lambda e: [op(e) for op in sch.ops["scalar"]])
                block.sync(lambda e: [op(e) for op in sch.ops["sync"]])
                block.gpsimd(lambda e: [op(e) for op in sch.ops["gpsimd"]])
            return nc

        xb = [af.take(1024) for _ in range(3)]
        junk = af.take(1024)
        xn = [ab.take(1024), ab.take(1024)]
        xnT_stage = [ab.take(4096) for _ in range(2)]
        stg_fm = [ab.take(4096), ab.take(4096)]
        stg_tm = [ab.take(4096), ab.take(4096)]

        tr_ps = ps[:, 0:512].bitcast(BF16)
        a2_ps = ps[:, 0:1024]
        fm_ps = [ps[:, 1024:1536], ps[:, 1536:2048]]
        tm_ps2 = [ps[:, 2048:3072], ps[:, 3072:4096]]
        rt_ps = ps[:, 512:544]

        xb_free = [() for _ in range(3)]
        xn_free = [() for _ in range(2)]
        stgfm_free = [(), ()]
        stgtm_free = [(), ()]
        stage_free = [(), ()]
        fm_free = [(), ()]
        tm_free2 = [(), ()]
        tm_i = 0
        tr_free = ()
        rt_free = ()
        fm_i = 0
        stgfm_i = 0
        junk_free = [()]
        tr_free = [()]
        INV_D = 1.0 / D

        tile_info = {}
        pending = None

        def emit_tile_front(j):
            sl = j % 3
            c0 = sl * 8
            ldt = sch.emit("sync", lambda e, j=j, sl=sl: e.dma_start(
                out=xb[sl], in_=xf[j * 128:(j + 1) * 128, :]),
                waits=xb_free[sl], inc=s_ld, inc_k=16)
            ms_tok = sch.emit("vector", lambda e, c0=c0: e.memset(
                stats[:, c0:c0 + 2], 0.0), inc=s_v)
            sm_tok = sch.emit("scalar", lambda e, sl=sl, c0=c0: e.activation(
                junk, xb[sl], AF.Identity, accum_out=stats[:, c0:c0 + 1]),
                waits=[ldt, ms_tok], hard_waits=list(junk_free[0]), inc=s_s)
            sq_tok = sch.emit("scalar", lambda e, sl=sl, c0=c0: e.activation(
                junk, xb[sl], AF.Square, accum_out=stats[:, c0 + 1:c0 + 2]),
                hard_waits=[sm_tok], inc=s_s)
            junk_free[0] = [sq_tok]
            sch.emit("vector", lambda e, c0=c0: e.tensor_scalar_mul(
                stats[:, c0 + 2:c0 + 3], stats[:, c0:c0 + 1], INV_D),
                waits=[sq_tok])
            t5 = sch.emit("vector", lambda e, c0=c0: e.tensor_scalar_mul(
                stats[:, c0 + 3:c0 + 4], stats[:, c0 + 1:c0 + 2], INV_D), inc=s_v)
            t6 = sch.emit("vector", lambda e, c0=c0: e.tensor_mul(
                stats[:, c0 + 4:c0 + 5], stats[:, c0 + 2:c0 + 3], stats[:, c0 + 2:c0 + 3]),
                hard_waits=[t5], inc=s_v)
            t7 = sch.emit("vector", lambda e, c0=c0: e.tensor_sub(
                stats[:, c0 + 4:c0 + 5], stats[:, c0 + 3:c0 + 4], stats[:, c0 + 4:c0 + 5]),
                hard_waits=[t6], inc=s_v)
            t8 = sch.emit("vector", lambda e, c0=c0: e.tensor_scalar(
                stats[:, c0 + 4:c0 + 5], stats[:, c0 + 4:c0 + 5], EPS, None, op0=ALU.add),
                hard_waits=[t7], inc=s_v)
            rec_tok = sch.emit("vector", lambda e, c0=c0: e.reciprocal(
                stats[:, c0 + 5:c0 + 6], stats[:, c0 + 4:c0 + 5]),
                hard_waits=[t8], inc=s_v)
            tile_info[j] = dict(sl=sl, c0=c0, rec=rec_tok)

        def emit_sqrt_nm(j):
            ti = tile_info[j]
            c0 = ti["c0"]
            sst_tok = sch.emit("scalar", lambda e, c0=c0: e.activation(
                stats[:, c0 + 6:c0 + 7], stats[:, c0 + 5:c0 + 6], AF.Sqrt),
                waits=[ti["rec"]], inc=s_s)
            nm_tok = sch.emit("vector", lambda e, c0=c0: e.scalar_tensor_tensor(
                stats[:, c0 + 7:c0 + 8], stats[:, c0 + 2:c0 + 3], -1.0,
                stats[:, c0 + 6:c0 + 7], op0=ALU.mult, op1=ALU.mult),
                waits=[sst_tok], inc=s_v)
            ti["nm"] = nm_tok

        def emit_tile_back(j):
            ti = tile_info[j]
            sl, c0 = ti["sl"], ti["c0"]
            tt = j % 4
            st_sl = (j // 4) % 2
            xsl = j % 2
            xn_tok = sch.emit("scalar", lambda e, sl=sl, c0=c0, xsl=xsl: e.activation(
                xn[xsl], xb[sl], AF.Identity,
                scale=stats[:, c0 + 6:c0 + 7], bias=stats[:, c0 + 7:c0 + 8]),
                waits=[ti["nm"]] + list(xn_free[xsl]), inc=s_s)
            xb_free[sl] = [xn_tok]

            def do_tr(e, xsl=xsl):
                r = None
                for dc in range(8):
                    r = e.transpose(tr_ps[:, dc * 128:(dc + 1) * 128],
                                    xn[xsl][:, dc * 128:(dc + 1) * 128], idb_sb)
                return r
            tr_tok = sch.emit("tensor", do_tr,
                              waits=[xn_tok, const_tok] + list(tr_free[0]),
                              inc=s_t)
            xn_free[xsl] = [tr_tok]
            ev_tok = sch.emit("vector", lambda e, st_sl=st_sl, tt=tt: e.tensor_copy(
                xnT_stage[st_sl].rearrange("p (dc t) -> p dc t", dc=8)[:, :, tt * 128:(tt + 1) * 128],
                tr_ps.rearrange("p (dc t) -> p dc t", dc=8)),
                waits=[tr_tok] + list(stage_free[st_sl] if tt == 0 else ()), inc=s_v)
            tr_free[0] = [ev_tok]
            ti["ev"] = ev_tok

        def emit_group_projs(g):
            nonlocal fm_i, tm_i, rt_free, sp_tok_last
            st_sl = g % 2
            stage_rdy = [tile_info[g * 4 + 3]["ev"]]
            for (wsb, dst, fsl) in (((ut_sb, zu_fm, 0), (vt_sb, zv_fm, 1)) if debug_stage >= 1 else ()):
                ev_toks_fm = []
                for fc in range(8):
                    pi = fm_i % 2

                    def do_fm(e, wsb=wsb, fc=fc, st_sl=st_sl, pi=pi):
                        r = None
                        for dc in range(8):
                            r = e.matmul(fm_ps[pi][:, 0:512],
                                         wsb[:, dc * 1024 + fc * 128: dc * 1024 + (fc + 1) * 128],
                                         xnT_stage[st_sl][:, dc * 512:(dc + 1) * 512],
                                         start=(dc == 0), stop=(dc == 7))
                        return r
                    mm_tok = sch.emit("tensor", do_fm,
                                      waits=stage_rdy + [wprep_tok] + list(fm_free[pi]), inc=s_t)
                    ev = sch.emit("vector", lambda e, fc=fc, pi=pi, fsl=fsl: e.tensor_scalar_mul(
                        stg_fm[fsl][:, fc * 512:(fc + 1) * 512], fm_ps[pi][:, 0:512], SCALE),
                        waits=[mm_tok] + (list(stgfm_free[fsl]) if fc == 0 else []), inc=s_v)
                    fm_free[pi] = [ev]
                    ev_toks_fm.append(ev)
                    fm_i += 1
                stok = sch.emit("sync", lambda e, dst=dst, g=g, fsl=fsl: e.dma_start(
                    out=dst[:, g * 512:(g + 1) * 512].rearrange("(fc p) t -> p fc t", p=128),
                    in_=stg_fm[fsl].rearrange("p (fc t) -> p fc t", fc=8)),
                    waits=[ev_toks_fm[-1]], inc=s_st, inc_k=16)
                stgfm_free[fsl] = [stok]

            for (wsb, dst, sti) in (((ut_sb, zu_tm, 0), (vt_sb, zv_tm, 1)) if debug_stage >= 1 else ()):
                ev_toks = []
                for tt in range(4):
                    tsl = tm_i % 2

                    def do_tm(e, wsb=wsb, st_sl=st_sl, tt=tt, tsl=tsl):
                        r = None
                        for fh in range(2):
                            for dc in range(8):
                                r = e.matmul(tm_ps2[tsl][:, fh * 512:(fh + 1) * 512],
                                             xnT_stage[st_sl][:, dc * 512 + tt * 128: dc * 512 + (tt + 1) * 128],
                                             wsb[:, dc * 1024 + fh * 512: dc * 1024 + (fh + 1) * 512],
                                             start=(dc == 0), stop=(dc == 7))
                        return r
                    mm_tok = sch.emit("tensor", do_tm,
                                      waits=stage_rdy + [wprep_tok] + list(tm_free2[tsl]) + list(stgtm_free[sti] if tt == 0 else ()),
                                      inc=s_t)
                    ev = sch.emit("vector", lambda e, sti=sti, tt=tt, tsl=tsl: e.tensor_scalar_mul(
                        stg_tm[sti][:, tt * 1024:(tt + 1) * 1024], tm_ps2[tsl], SCALE),
                        waits=[mm_tok], inc=s_v)
                    tm_free2[tsl] = [ev]
                    tm_i += 1
                    ev_toks.append(ev)
                stok = sch.emit("sync", lambda e, sti=sti, dst=dst, g=g: e.dma_start(
                    out=dst[g * 512:(g + 1) * 512, :].rearrange("(tt p) f -> p tt f", p=128),
                    in_=stg_tm[sti].rearrange("p (tt f) -> p tt f", tt=4)),
                    waits=ev_toks, inc=s_st, inc_k=16)
                stgtm_free[sti] = [stok]

            if g < OWN_G:
                sp_tok = sch.emit("sync", lambda e, st_sl=st_sl, g=g: e.dma_start(
                    out=xnT_hb[:, g * 512:(g + 1) * 512].rearrange("(dc p) t -> p dc t", p=128),
                    in_=xnT_stage[st_sl].rearrange("p (dc t) -> p dc t", dc=8)),
                    waits=stage_rdy, inc=s_st, inc_k=16)
                for tt in (range(4) if debug_stage >= 1 else ()):
                    j = g * 4 + tt

                    def do_rt(e, st_sl=st_sl, tt=tt):
                        r = None
                        for dc in range(8):
                            r = e.matmul(rt_ps[:, tt * 8:(tt + 1) * 8],
                                         xnT_stage[st_sl][:, dc * 512 + tt * 128: dc * 512 + (tt + 1) * 128],
                                         wr_sb[:, dc * 8:(dc + 1) * 8],
                                         start=(dc == 0), stop=(dc == 7))
                        return r
                    rt_tok = sch.emit("tensor", do_rt,
                                      waits=stage_rdy + [wprep_tok] + list(rt_free), inc=s_t)
                    r1 = sch.emit("vector", lambda e, tt=tt: e.tensor_scalar_mul(
                        rtr[:, 0:8], rt_ps[:, tt * 8:(tt + 1) * 8], SCALE),
                        waits=[rt_tok, const_tok], inc=s_v)
                    rt_free = [r1]
                    r2 = sch.emit("vector", lambda e: e.tensor_add(rtr[:, 8:16], rtr[:, 0:8], rb_sb),
                                  hard_waits=[r1], inc=s_v)
                    r3 = sch.emit("vector", lambda e: e.reduce_max(rtr[:, 40:41], rtr[:, 8:16], axis=AX),
                                  hard_waits=[r2], inc=s_v)
                    r4 = sch.emit("vector", lambda e: e.tensor_scalar(
                        rmask, rtr[:, 8:16], rtr[:, 40:41], None, op0=ALU.is_lt),
                        hard_waits=[r3], inc=s_v)
                    r5 = sch.emit("vector", lambda e: e.select(
                        rtr[:, 16:24], rmask, rtr[:, 8:16], negbig, add_drain=True),
                        hard_waits=[r4], inc=s_v)
                    r7 = sch.emit("vector", lambda e: e.reduce_max(rtr[:, 41:42], rtr[:, 16:24], axis=AX),
                                  hard_waits=[r5], inc=s_v)
                    r8 = sch.emit("vector", lambda e: e.tensor_scalar(
                        rmask, rtr[:, 8:16], rtr[:, 41:42], None, op0=ALU.is_lt),
                        hard_waits=[r7], inc=s_v)
                    r9 = sch.emit("vector", lambda e: e.select(
                        rtr[:, 24:32], rmask, negbig, rtr[:, 0:8], add_drain=True),
                        hard_waits=[r8], inc=s_v)
                    r10 = sch.emit("vector", lambda e: e.reduce_max(rtr[:, 42:43], rtr[:, 24:32], axis=AX),
                                   hard_waits=[r9], inc=s_v)
                    zt_tok = sch.emit("vector", lambda e: e.tensor_scalar_mul(
                        rtr[:, 43:44], rtr[:, 42:43], -1.0), hard_waits=[r10], inc=s_v)
                    ex_tok = sch.emit("scalar", lambda e: e.activation(
                        rtr[:, 32:40], rtr[:, 24:32], AF.Exp, bias=rtr[:, 43:44]),
                        waits=[zt_tok], inc=s_s)
                    r12 = sch.emit("vector", lambda e: e.reduce_sum(rtr[:, 44:45], rtr[:, 32:40], axis=AX),
                                   waits=[ex_tok], inc=s_v)
                    r13 = sch.emit("vector", lambda e: e.reciprocal(rtr[:, 45:46], rtr[:, 44:45]),
                                   hard_waits=[r12], inc=s_v)
                    sch.emit("vector", lambda e, j=j: e.tensor_scalar_mul(
                        wfull_sb[:, j * 8:(j + 1) * 8], rtr[:, 32:40], rtr[:, 45:46]),
                        hard_waits=[r13], inc=s_v)

            stage_free[st_sl] = [sch.tok(s_t)] + ([sp_tok] if g < OWN_G else [])

        sp_tok_last = None
        for j in range(A_NG * 4):
            emit_tile_front(j)
            if pending is not None:
                emit_tile_back(pending)
                if pending % 4 == 3:
                    emit_group_projs(pending // 4)
            pending = j
            emit_sqrt_nm(j)
        emit_tile_back(pending)
        emit_group_projs(A_NG - 1)

        wf_tok = sch.tok(s_v)
        for j in range(OWN_T):
            t_tok = sch.emit("tensor", lambda e, j=j: e.transpose(
                a2_ps[0:8, 0:128], wfull_sb[:, j * 8:(j + 1) * 8], idf_sb),
                waits=[wf_tok] + list(tr_free[0]), inc=s_t)
            ev = sch.emit("vector", lambda e, j=j: e.tensor_copy(
                wT_sb[0:8, j * 128:(j + 1) * 128], a2_ps[0:8, 0:128]),
                waits=[t_tok], inc=s_v)
            tr_free[0] = [ev]

        qh1_sb = qh2_sb = None
        if debug_stage >= 2:
            fence_a = sch.fence()
            ab.reset_to(0)
            af.reset_to(0)
            qh1_sb = ab.take(8 * OWN)
            qh2_sb = ab.take(8 * OWN)
            fmbuf = ab.take(S)
            ownbuf = ab.take(OWN)
            tmbuf = [ab.take(4096), ab.take(4096)]
            A_sb2 = [[ab.take(1024), ab.take(1024)], [ab.take(1024), ab.take(1024)]]
            STRIP_END = ab.off

            psc = [ps[:, 0:1024], ps[:, 1024:2048]]
            qh_ps = ps[:, 2048:4096]

            slot_free = [list(fence_a), list(fence_a)]
            tmb_free = [list(fence_a), list(fence_a)]
            last_sig = {0: list(fence_a), 1: list(fence_a)}
            last_acc = {(p, b): () for p in range(2) for b in range(2)}
            qh_evac = list(fence_a)

            def strip(fm_src, tm_src, own_src, qh_out):
                nonlocal qh_evac
                sig_tok = {}
                acc_tok = {}
                for c in range(STRIP_C):
                    sl = c % 2
                    r0 = 64 * sl
                    fm_t = sch.emit("sync", lambda e, c=c, r0=r0, fm_src=fm_src: e.dma_start(
                        out=fmbuf[r0:r0 + 64, :], in_=fm_src[c * 64:(c + 1) * 64, :]),
                        waits=slot_free[sl], inc=s_ld, inc_k=16)
                    own_t = sch.emit("sync", lambda e, c=c, r0=r0, own_src=own_src: e.dma_start(
                        out=ownbuf[r0:r0 + 64, :], in_=own_src[c * 64:(c + 1) * 64, 0:OWN]),
                        waits=slot_free[sl], inc=s_ld, inc_k=16)
                    tm_t = {}
                    for jj in range(8):
                        tm_t[jj] = sch.emit("sync", lambda e, c=c, jj=jj, sl=sl, tm_src=tm_src: e.dma_start(
                            out=tmbuf[sl][:, jj * 512:(jj + 1) * 512],
                            in_=tm_src[jj * 1024:(jj + 1) * 1024, c * 64:(c + 1) * 64]
                            .rearrange("(j p) f -> p j f", p=128)),
                            waits=tmb_free[sl], inc=s_ld, inc_k=16)
                    loads = [fm_t, own_t]

                    def emit_acc(t, only_b, c=c, sl=sl):
                        tp2 = t % 2
                        for b in (only_b,):
                            def do_acc(e, c=c, sl=sl, t=t, b=b, tp2=tp2):
                                r = None
                                p0 = 64 * (c % 2)
                                for sg in range(2):
                                    sgg = b * 2 + sg
                                    r = e.matmul(qh_ps[p0:p0 + 64, sgg * 512:(sgg + 1) * 512],
                                                 tmbuf[sl][:, t * 64:(t + 1) * 64],
                                                 A_sb2[tp2][b][:, sg * 512:(sg + 1) * 512],
                                                 start=(t == 0), stop=(t == STRIP_T - 1),
                                                 tile_position=(0, p0))
                                return r
                            aw = [sig_tok[(t, b)], tm_t[t // 8]]
                            if t == 0 and sl == 0:
                                aw = aw + qh_evac
                            acc_tok[(t, b)] = sch.emit("tensor", do_acc, waits=aw, inc=s_t)
                            last_acc[(tp2, b)] = [acc_tok[(t, b)]]

                    for t in range(STRIP_T):
                        tp2 = t % 2
                        for b in range(2):
                            def do_sc(e, r0=r0, t=t, b=b):
                                r = None
                                for sg in range(2):
                                    sgg = b * 2 + sg
                                    r = e.matmul(psc[b][:, sg * 512:(sg + 1) * 512],
                                                 fmbuf[r0:r0 + 64, t * 128:(t + 1) * 128],
                                                 ownbuf[r0:r0 + 64, sgg * 512:(sgg + 1) * 512],
                                                 start=True, stop=True)
                                return r
                            sc_tok = sch.emit("tensor", do_sc,
                                              waits=loads + list(last_sig[b]), inc=s_t)
                            sig_tok[(t, b)] = sch.emit("scalar", lambda e, b=b, tp2=tp2: e.activation(
                                A_sb2[tp2][b], psc[b], AF.Sigmoid),
                                waits=[sc_tok] + list(last_acc[(tp2, b)]), inc=s_s)
                            last_sig[b] = [sig_tok[(t, b)]]
                            if t > 0:
                                emit_acc(t - 1, only_b=b)
                    emit_acc(STRIP_T - 1, only_b=0)
                    emit_acc(STRIP_T - 1, only_b=1)
                    slot_free[sl] = [acc_tok[(STRIP_T - 1, 1)]]
                    tmb_free[sl] = [acc_tok[(STRIP_T - 1, 1)]]
                    if sl == 1:
                        ev = sch.emit("vector", lambda e, c=c, qh_out=qh_out: e.tensor_scalar_mul(
                            qh_out[:, (c // 2) * OWN:(c // 2 + 1) * OWN], qh_ps, SCALE),
                            waits=[acc_tok[(STRIP_T - 1, 0)], acc_tok[(STRIP_T - 1, 1)]], inc=s_v)
                        qh_evac = [ev]

            strip(zv_fm, zv_tm, zu_fm, qh1_sb)
            strip(zu_fm, zu_tm, zv_fm, qh2_sb)

        a_done_tok = sch.tok(s_ld)
        first_pc = [True]
        for ei_ in range(E):
            for (dst_, src_) in ((w1b, w1), (w2b, w2), (w3b, w3)):
                sch.emit("gpsimd", lambda e, dst_=dst_, src_=src_, ei_=ei_: e.dma_start(
                    out=dst_[ei_], in_=src_[ei_]),
                    waits=([a_done_tok] if first_pc[0] else []), inc=s_w, inc_k=16)
                first_pc[0] = False
        for (dst_, src_) in ((ws1b, ws1), (ws2b, ws2), (ws3b, ws3), (ub, u_in), (vb, v_in)):
            sch.emit("gpsimd", lambda e, dst_=dst_, src_=src_: e.dma_start(
                out=dst_[:, :], in_=src_[:, :]), inc=s_w, inc_k=16)
        precast_tok = sch.tok(s_w)


        if debug_stage >= 3:
            fence_s = sch.fence()
            ab.reset_to(32768)
            u_sb = ab.take(8192)
            v_sb = ab.take(8192)
            for fc in range(8):
                sch.emit("sync", lambda e, fc=fc: e.dma_start(
                    out=u_sb[:, fc * 1024:(fc + 1) * 1024], in_=ub[fc * 128:(fc + 1) * 128, :]),
                    waits=fence_s + [precast_tok], inc=s_ld, inc_k=16)
                sch.emit("sync", lambda e, fc=fc: e.dma_start(
                    out=v_sb[:, fc * 1024:(fc + 1) * 1024], in_=vb[fc * 128:(fc + 1) * 128, :]),
                    waits=fence_s + [precast_tok], inc=s_ld, inc_k=16)
            uv_tok = sch.tok(s_ld)

            g_ps = [ps[:, 0:512], ps[:, 512:1024]]
            g_free = [list(fence_s), list(fence_s)]
            gi = 0
            for dc in range(8):
                for sg in range(4):
                    pi = gi % 2

                    def do_g(e, dc=dc, sg=sg, pi=pi):
                        r = None
                        for fc in range(8):
                            e.matmul(g_ps[pi], u_sb[:, fc * 1024 + dc * 128: fc * 1024 + (dc + 1) * 128],
                                     qh1_sb[:, fc * OWN + sg * 512: fc * OWN + (sg + 1) * 512],
                                     start=(fc == 0), stop=False)
                        for fc in range(8):
                            r = e.matmul(g_ps[pi], v_sb[:, fc * 1024 + dc * 128: fc * 1024 + (dc + 1) * 128],
                                         qh2_sb[:, fc * OWN + sg * 512: fc * OWN + (sg + 1) * 512],
                                         start=False, stop=(fc == 7))
                        return r
                    mm = sch.emit("tensor", do_g, waits=[uv_tok] + fence_s + g_free[pi], inc=s_t)
                    ev = sch.emit("vector", lambda e, dc=dc, sg=sg, pi=pi: e.tensor_copy(
                        acc[:, dc * OWN + sg * 512: dc * OWN + (sg + 1) * 512], g_ps[pi]),
                        waits=[mm], inc=s_v)
                    g_free[pi] = [ev]
                    gi += 1

        if debug_stage >= 4:
            emit_ffn_phase(nc, sch, (s_t, s_v, s_s, s_ld, s_st, s_w), ab, af, ps, dict(
                acc=acc, sel_sb=sel_sb, wT_sb=wT_sb, xnT_hb=xnT_hb, const_tok=const_tok,
                w1b=w1b, w2b=w2b, w3b=w3b, ws1b=ws1b, ws2b=ws2b, ws3b=ws3b,
                precast_tok=precast_tok))

            fence_f = sch.fence()
            if not RESID:
                fence_f = fence_f
            af.reset_to(2048)
            xb2 = [af.take(1024), af.take(1024)]
            ob = [af.take(1024), af.take(1024)]
            tr_free2 = list(fence_f)
            xb2_free = [list(fence_f), list(fence_f)]
            ob_free = [list(fence_f), list(fence_f)]
            for j in (range(OWN_T) if RESID else ()):
                sl = j % 2
                x_tok = sch.emit("sync", lambda e, j=j, sl=sl: e.dma_start(
                    out=xb2[sl], in_=xf[j * 128:(j + 1) * 128, :]),
                    waits=xb2_free[sl], inc=s_ld, inc_k=16)

                def do_trr(e, j=j):
                    r = None
                    for dc in range(8):
                        r = e.transpose(ps[:, dc * 128:(dc + 1) * 128],
                                        acc[:, dc * OWN + j * 128: dc * OWN + (j + 1) * 128],
                                        idf_sb)
                    return r
                tr_tok = sch.emit("tensor", do_trr, waits=fence_f + tr_free2, inc=s_t)
                o_tok = sch.emit("vector", lambda e, sl=sl: e.tensor_add(
                    ob[sl], ps[:, 0:1024], xb2[sl]),
                    waits=[tr_tok, x_tok] + ob_free[sl], inc=s_v)
                tr_free2 = [o_tok]
                xb2_free[sl] = [o_tok]
                st_tok = sch.emit("sync", lambda e, j=j, sl=sl: e.dma_start(
                    out=out[j * 128:(j + 1) * 128, :], in_=ob[sl]),
                    waits=[o_tok], inc=s_st, inc_k=16)
                ob_free[sl] = [st_tok]

        if debug_stage < 4:
            fence = sch.fence()
            pairs = [("zu_fm", zu_fm), ("zv_fm", zv_fm), ("zu_tm", zu_tm), ("xnT", xnT_hb)]
            for (k, src) in pairs:
                sch.emit("sync", lambda e, k=k, src=src: e.dma_start(out=dbg[k][:, :], in_=src[:, :]),
                         waits=fence, inc=s_st, inc_k=16)
            sch.emit("sync", lambda e: e.dma_start(out=dbg["wfull"][:, :], in_=wfull_sb),
                     waits=fence, inc=s_st, inc_k=16)
            if debug_stage >= 2:
                sch.emit("sync", lambda e: e.dma_start(out=dbg["qh1"][:, :], in_=qh1_sb),
                         waits=fence, inc=s_st, inc_k=16)
                sch.emit("sync", lambda e: e.dma_start(out=dbg["qh2"][:, :], in_=qh2_sb),
                         waits=fence, inc=s_st, inc_k=16)
            if debug_stage >= 3:
                sch.emit("sync", lambda e: e.dma_start(out=dbg["acc"][:, :], in_=acc),
                         waits=fence, inc=s_st, inc_k=16)

        sch.emit("sync", waits=[sch.tok(s_st), sch.tok(s_ld), sch.tok(s_w)])
        sch.finalize()

        with nc.Block() as block:
            block.tensor(lambda e: [op(e) for op in sch.ops["tensor"]])
            block.vector(lambda e: [op(e) for op in sch.ops["vector"]])
            block.scalar(lambda e: [op(e) for op in sch.ops["scalar"]])
            block.sync(lambda e: [op(e) for op in sch.ops["sync"]])
            block.gpsimd(lambda e: [op(e) for op in sch.ops["gpsimd"]])

    return nc


def kernel(L0_attn_qz, L0_attn_ternary_factor_u, L0_attn_ternary_factor_v,
           L0_ffn_Wr, L0_ffn_rb, L0_ffn_W1, L0_ffn_W2, L0_ffn_W3,
           L0_ffn_Ws1, L0_ffn_Ws2, L0_ffn_Ws3):
    x = np.ascontiguousarray(np.asarray(L0_attn_qz, dtype=np.float32))
    u = np.ascontiguousarray(np.asarray(L0_attn_ternary_factor_u, dtype=np.float32))
    v = np.ascontiguousarray(np.asarray(L0_attn_ternary_factor_v, dtype=np.float32))
    wr_np = np.ascontiguousarray(np.asarray(L0_ffn_Wr, dtype=np.float32))
    rb_np = np.asarray(L0_ffn_rb, dtype=np.float32)
    w1_np = np.ascontiguousarray(np.asarray(L0_ffn_W1, dtype=np.float32))
    w2_np = np.ascontiguousarray(np.asarray(L0_ffn_W2, dtype=np.float32))
    w3_np = np.ascontiguousarray(np.asarray(L0_ffn_W3, dtype=np.float32))
    ws1_np = np.ascontiguousarray(np.asarray(L0_ffn_Ws1, dtype=np.float32))
    ws2_np = np.ascontiguousarray(np.asarray(L0_ffn_Ws2, dtype=np.float32))
    ws3_np = np.ascontiguousarray(np.asarray(L0_ffn_Ws3, dtype=np.float32))

    ut_np = np.ascontiguousarray(u.T)
    vt_np = np.ascontiguousarray(v.T)
    rb_rep = np.ascontiguousarray(np.broadcast_to(rb_np[None, :], (128, E)))

    nc = build_nc(DEBUG_STAGE)

    in_maps = []
    for k in range(8):
        b, q = k // 4, k % 4
        idx = np.r_[q * OWN:(q + 1) * OWN, 0:q * OWN, (q + 1) * OWN:S]
        xfk = np.ascontiguousarray(x[b][idx])
        in_maps.append({
            "xf": xfk, "ut": ut_np, "vt": vt_np, "u_in": u, "v_in": v,
            "wr": wr_np, "rb_rep": rb_rep,
            "w1": w1_np, "w2": w2_np, "w3": w3_np,
            "ws1": ws1_np, "ws2": ws2_np, "ws3": ws3_np,
        })

    res = run_bass_kernel_spmd(nc, in_maps, core_ids=list(range(8)))
    kernel._last_results = res

    outp = np.empty((2, S, D), dtype=np.float32)
    for k in range(8):
        b, q = k // 4, k % 4
        outp[b, q * OWN:(q + 1) * OWN] = res.results[k]["out"]
    return outp


# version 10
# speedup vs baseline: 1.0394x; 1.0394x over previous
import numpy as np
import ml_dtypes
from contextlib import ExitStack

import concourse.bass as bass
import concourse.mybir as mybir
from concourse.bass_utils import run_bass_kernel_spmd

F32 = mybir.dt.float32
BF16 = mybir.dt.bfloat16
AF = mybir.ActivationFunctionType
ALU = mybir.AluOpType
AX = mybir.AxisListType.X
BF = ml_dtypes.bfloat16

S = 8192
OWN = 2048
D = 1024
C = 16
R = 64
E = 8
H = 2048
SCALE = 0.03125
EPS = 1e-5
BIG = 50.0
NG = S // 512
OWN_G = OWN // 512
OWN_T = OWN // 128

DEBUG_STAGE = 4
A_NG = NG
STRIP_C = C
STRIP_T = 64
FFN_M = 2
FFN_EI = 9
FFN_EIS = None
RESID = True
FFN_W13 = True
FFN_W2 = True

ENGS = ("tensor", "vector", "scalar", "sync", "gpsimd")


class Sched:
    def __init__(self, sems):
        self.ops = {e: [] for e in ENGS}
        self.cnt = {s: 0 for s in sems}
        self.waited = {e: {} for e in ENGS}
        self.owner = {}
        self.all_waits = {}
        self.dma_emits = []

    def own(self, sem, eng):
        self.owner[sem] = eng

    def tok(self, sem):
        return (sem, self.cnt[sem])

    def emit(self, eng, fn=None, waits=(), inc=None, inc_k=1, hard_waits=()):
        wl = []
        for (s, v), hard in [(w, True) for w in hard_waits] + [(w, False) for w in waits]:
            if v <= 0:
                continue
            assert self.cnt[s] >= v, f"wait for {v} but only {self.cnt[s]} emitted"
            self.all_waits.setdefault(s, set()).add(v)
            if self.owner.get(s) == eng and not hard:
                continue
            if self.waited[eng].get(s, 0) >= v:
                continue
            self.waited[eng][s] = v
            wl.append((s, v))
        tok = None
        prewait = [None]
        if inc is not None:
            if inc_k % 16 == 0:
                self.dma_emits.append((eng, len(self.ops[eng]), inc, self.cnt[inc], prewait))
            self.cnt[inc] += inc_k
            tok = (inc, self.cnt[inc])

        def op(e, fn=fn, wl=tuple(wl), inc=inc, inc_k=inc_k, prewait=prewait):
            if prewait[0] is not None:
                e.wait_ge(*prewait[0])
            for (s, v) in wl:
                e.wait_ge(s, v)
            r = fn(e) if fn is not None else None
            if inc is not None and r != "self_inc":
                if r is None:
                    r = e.nop()
                r.then_inc(inc, inc_k)
        op.wl = wl
        self.ops[eng].append(op)
        return tok

    def finalize(self):
        sorted_waits = {s: sorted(vs) for s, vs in self.all_waits.items()}
        import bisect
        want = {}
        for (eng, idx, sem, cnt_before, prewait) in self.dma_emits:
            vs = sorted_waits.get(sem)
            if not vs:
                continue
            i = bisect.bisect_right(vs, cnt_before)
            if i == 0:
                continue
            want[(eng, idx)] = (sem, vs[i - 1], prewait)
        for eng in ENGS:
            cur = {}
            for idx, op in enumerate(self.ops[eng]):
                w = want.get((eng, idx))
                if w is not None:
                    sem, b, prewait = w
                    if cur.get(sem, 0) < b:
                        prewait[0] = (sem, b)
                        cur[sem] = b
                for (s, v) in op.wl:
                    cur[s] = max(cur.get(s, 0), v)

    def fence(self):
        return [self.tok(s) for s in self.cnt]


class Arena:
    def __init__(self, ap, total_cols):
        self.ap = ap
        self.total = total_cols
        self.off = 0

    def take(self, cols):
        cols = (cols + 15) & ~15
        assert self.off + cols <= self.total, f"arena overflow {self.off}+{cols}>{self.total}"
        sl = self.ap[:, self.off:self.off + cols]
        self.off += cols
        return sl

    def reset_to(self, off=0):
        self.off = off




def emit_ffn_phase(nc, sch, sems, ab, af, ps, env):
    s_t, s_v, s_s, s_ld, s_st, s_w = sems
    acc = env["acc"]; sel_sb = env["sel_sb"]; wT_sb = env["wT_sb"]
    xnT_hb = env["xnT_hb"]; const_tok = env["const_tok"]
    w1b = env["w1b"]; w2b = env["w2b"]; w3b = env["w3b"]
    ws1b = env["ws1b"]; ws2b = env["ws2b"]; ws3b = env["ws3b"]
    precast_tok = env["precast_tok"]
    fence_g = sch.fence()
    ab.reset_to(0)
    af.reset_to(0)
    xnT_own = ab.take(8 * OWN)
    h_sb = ab.take(16 * 1024)
    w13 = [ab.take(4096), ab.take(4096)]
    w2sl = [ab.take(4096), ab.take(4096)]
    sl_sb = [ab.take(1024), ab.take(1024)]
    wbc = af.take(1024)
    t_sb = [af.take(512), af.take(512)]

    for dc in range(8):
        sch.emit("sync", lambda e, dc=dc: e.dma_start(
            out=xnT_own[:, dc * OWN:(dc + 1) * OWN], in_=xnT_hb[dc * 128:(dc + 1) * 128, :]),
            waits=fence_g, inc=s_ld, inc_k=16)
    xnT_tok = sch.tok(s_ld)

    ps1 = [ps[:, 0:1024], ps[:, 1024:2048]]
    ps3 = [ps[:, 2048:3072], ps[:, 3072:4096]]
    w13_free = [list(fence_g), list(fence_g)]
    w2_free = [list(fence_g), list(fence_g)]
    sl_free = [list(fence_g), list(fence_g)]
    comb_tok = {dc: list(fence_g) for dc in range(8)}
    h_tok = {}
    rnd = 0
    expert_fence = list(fence_g)

    for m in range(FFN_M):
        for ei in (FFN_EIS if FFN_EIS is not None else range(FFN_EI)):
            shared = (ei == 8)
            wbc_toks = []
            if not shared:
                for sg in range(2):
                    sgg = m * 2 + sg
                    bmm = sch.emit("tensor", lambda e, ei=ei, sgg=sgg, sg=sg: e.matmul(
                        ps[:, sg * 512:(sg + 1) * 512], sel_sb[:, ei * 128:(ei + 1) * 128],
                        wT_sb[:, sgg * 512:(sgg + 1) * 512],
                        start=True, stop=True),
                        waits=[const_tok] + expert_fence, inc=s_t)
                    wv = sch.emit("vector", lambda e, sg=sg: e.tensor_copy(
                        wbc[:, sg * 512:(sg + 1) * 512], ps[:, sg * 512:(sg + 1) * 512]),
                        waits=[bmm], inc=s_v)
                    wbc_toks.append(wv)
            first_rnd = rnd
            for hgg in (range(8) if FFN_W13 else ()):
                wsl = hgg % 2
                h0 = hgg * 256
                srcs = (ws1b, ws3b) if shared else (w1b[ei], w3b[ei])

                def ld13(e, wsl=wsl, h0=h0, srcs=srcs):
                    o3 = w13[wsl].rearrange("p (dc x) -> p dc x", dc=8)
                    e.dma_start(out=o3[:, :, 0:256],
                                in_=srcs[0].rearrange("(dc p) h -> p dc h", p=128)[:, :, h0:h0 + 256]).then_inc(s_ld, 16)
                    e.dma_start(out=o3[:, :, 256:512],
                                in_=srcs[1].rearrange("(dc p) h -> p dc h", p=128)[:, :, h0:h0 + 256]).then_inc(s_ld, 16)
                    return "self_inc"
                ldt = sch.emit("sync", ld13, waits=w13_free[wsl] + [precast_tok], inc=s_ld, inc_k=16 * 2)
                for sg in range(2):
                    pp = rnd % 2
                    sgg = m * 2 + sg

                    def do_w13(e, wsl=wsl, sgg=sgg, pp=pp):
                        r = None
                        for hc in range(2):
                            for dc in range(8):
                                e.matmul(ps1[pp][:, hc * 512:(hc + 1) * 512],
                                         w13[wsl][:, dc * 512 + hc * 128: dc * 512 + (hc + 1) * 128],
                                         xnT_own[:, dc * OWN + sgg * 512: dc * OWN + (sgg + 1) * 512],
                                         start=(dc == 0), stop=(dc == 7))
                        for hc in range(2):
                            for dc in range(8):
                                r = e.matmul(ps3[pp][:, hc * 512:(hc + 1) * 512],
                                             w13[wsl][:, dc * 512 + 256 + hc * 128: dc * 512 + 256 + (hc + 1) * 128],
                                             xnT_own[:, dc * OWN + sgg * 512: dc * OWN + (sgg + 1) * 512],
                                             start=(dc == 0), stop=(dc == 7))
                        return r
                    mm_w = [ldt, xnT_tok] + h_tok.get(rnd - 2, [])
                    if rnd - first_rnd < 2:
                        mm_w = mm_w + expert_fence + wbc_toks
                    mm_tok = sch.emit("tensor", do_w13, waits=mm_w, inc=s_t)
                    si_tok = sch.emit("scalar", lambda e, pp=pp, wsl=wsl: e.activation(
                        sl_sb[wsl], ps3[pp], AF.Silu),
                        waits=[mm_tok] + sl_free[wsl], inc=s_s)
                    mu_tok = sch.emit("vector", lambda e, pp=pp, wsl=wsl, hgg=hgg, sg=sg: e.tensor_mul(
                        h_sb.rearrange("p (hc s) -> p hc s", hc=16)[:, hgg * 2:hgg * 2 + 2, sg * 512:(sg + 1) * 512],
                        ps1[pp].rearrange("p (hc s) -> p hc s", hc=2),
                        sl_sb[wsl].rearrange("p (hc s) -> p hc s", hc=2)),
                        waits=[si_tok], inc=s_v)
                    sl_free[wsl] = [mu_tok]
                    h_tok[rnd] = [mu_tok]
                    rnd += 1
                w13_free[wsl] = h_tok[rnd - 1]
            h_all = h_tok.get(rnd - 1, list(fence_g))
            mm2_dc = {}
            prev_cv2 = {0: (), 1: ()}
            for sg in (range(2) if FFN_W2 else ()):
                sgg = m * 2 + sg
                for hg in range(4):
                    wsl = hg % 2
                    w2src = ws2b if shared else w2b[ei]

                    def ldw2(e, wsl=wsl, hg=hg, w2src=w2src):
                        return e.dma_start(
                            out=w2sl[wsl].rearrange("p (hc d) -> p hc d", hc=4),
                            in_=w2src[hg * 512:(hg + 1) * 512, :].rearrange("(hc p) d -> p hc d", p=128))
                    w2t = sch.emit("sync", ldw2, waits=w2_free[wsl] + [precast_tok], inc=s_ld, inc_k=16)
                    for dc in range(8):
                        def do_w2(e, sg=sg, hg=hg, wsl=wsl, dc=dc):
                            r = None
                            for hc in range(4):
                                r = e.matmul(ps[:, dc * 512:(dc + 1) * 512],
                                             w2sl[wsl][:, hc * 1024 + dc * 128: hc * 1024 + (dc + 1) * 128],
                                             h_sb[:, (hg * 4 + hc) * 1024 + sg * 512: (hg * 4 + hc) * 1024 + (sg + 1) * 512],
                                             start=(hg == 0 and hc == 0), stop=(hg == 3 and hc == 3))
                            return r
                        ww = [w2t] + h_all
                        if hg == 0:
                            ww = ww + comb_tok[dc]
                        mm2_dc[dc] = sch.emit("tensor", do_w2, waits=ww, inc=s_t)
                    w2_free[wsl] = [mm2_dc[7]]
                for dc in range(8):
                    if shared:
                        cv = sch.emit("vector", lambda e, dc=dc, sgg=sgg: e.tensor_add(
                            acc[:, dc * OWN + sgg * 512: dc * OWN + (sgg + 1) * 512],
                            acc[:, dc * OWN + sgg * 512: dc * OWN + (sgg + 1) * 512],
                            ps[:, dc * 512:(dc + 1) * 512]),
                            waits=[mm2_dc[dc]], inc=s_v)
                    else:
                        mt = sch.emit("vector", lambda e, dc=dc, sg=sg: e.tensor_mul(
                            t_sb[dc % 2], ps[:, dc * 512:(dc + 1) * 512], wbc[:, sg * 512:(sg + 1) * 512]),
                            waits=[mm2_dc[dc]],
                            hard_waits=list(prev_cv2[dc % 2]), inc=s_v)
                        cv = sch.emit("vector", lambda e, dc=dc, sgg=sgg: e.tensor_add(
                            acc[:, dc * OWN + sgg * 512: dc * OWN + (sgg + 1) * 512],
                            acc[:, dc * OWN + sgg * 512: dc * OWN + (sgg + 1) * 512], t_sb[dc % 2]),
                            hard_waits=[mt], inc=s_v)
                        prev_cv2[dc % 2] = [cv]
                    comb_tok[dc] = [cv]
            expert_fence = comb_tok[7]



def build_nc(debug_stage=4):
    nc = bass.Bass("TRN2", target_bir_lowering=False)

    xf = nc.dram_tensor("xf", [S, D], F32, kind="ExternalInput").ap()
    ut = nc.dram_tensor("ut", [D, D], F32, kind="ExternalInput").ap()
    vt = nc.dram_tensor("vt", [D, D], F32, kind="ExternalInput").ap()
    u_in = nc.dram_tensor("u_in", [D, D], F32, kind="ExternalInput").ap()
    v_in = nc.dram_tensor("v_in", [D, D], F32, kind="ExternalInput").ap()
    wr = nc.dram_tensor("wr", [D, E], F32, kind="ExternalInput").ap()
    rb_rep = nc.dram_tensor("rb_rep", [128, E], F32, kind="ExternalInput").ap()
    w1 = nc.dram_tensor("w1", [E, D, H], F32, kind="ExternalInput").ap()
    w2 = nc.dram_tensor("w2", [E, H, D], F32, kind="ExternalInput").ap()
    w3 = nc.dram_tensor("w3", [E, D, H], F32, kind="ExternalInput").ap()
    ws1 = nc.dram_tensor("ws1", [D, H], F32, kind="ExternalInput").ap()
    ws2 = nc.dram_tensor("ws2", [H, D], F32, kind="ExternalInput").ap()
    ws3 = nc.dram_tensor("ws3", [D, H], F32, kind="ExternalInput").ap()

    out = nc.dram_tensor("out", [OWN, D], F32, kind="ExternalOutput").ap()

    zu_fm = nc.dram_tensor("zu_fm", [D, S], BF16, kind="Internal").ap()
    zv_fm = nc.dram_tensor("zv_fm", [D, S], BF16, kind="Internal").ap()
    zu_tm = nc.dram_tensor("zu_tm", [S, D], BF16, kind="Internal").ap()
    zv_tm = nc.dram_tensor("zv_tm", [S, D], BF16, kind="Internal").ap()
    xnT_hb = nc.dram_tensor("xnT_hb", [D, OWN], BF16, kind="Internal").ap()
    w1b = nc.dram_tensor("w1b", [E, D, H], BF16, kind="Internal").ap()
    w2b = nc.dram_tensor("w2b", [E, H, D], BF16, kind="Internal").ap()
    w3b = nc.dram_tensor("w3b", [E, D, H], BF16, kind="Internal").ap()
    ws1b = nc.dram_tensor("ws1b", [D, H], BF16, kind="Internal").ap()
    ws2b = nc.dram_tensor("ws2b", [H, D], BF16, kind="Internal").ap()
    ws3b = nc.dram_tensor("ws3b", [D, H], BF16, kind="Internal").ap()
    ub = nc.dram_tensor("ub", [D, D], BF16, kind="Internal").ap()
    vb = nc.dram_tensor("vb", [D, D], BF16, kind="Internal").ap()

    identf = nc.inline_tensor(np.eye(128, dtype=np.float32), "identf").ap()
    identb = nc.inline_tensor(np.eye(128, dtype=BF), "identb").ap()
    sel_np = np.zeros((8, 8 * 128), dtype=BF)
    for e_ in range(8):
        sel_np[e_, e_ * 128:(e_ + 1) * 128] = 1.0
    sel_c = nc.inline_tensor(sel_np, "sel_c").ap()

    tb = {}
    if debug_stage == 5:
        tb["xnT_in"] = nc.dram_tensor("xnT_in", [D, OWN], BF16, kind="ExternalInput").ap()
        tb["wT_in"] = nc.dram_tensor("wT_in", [8, OWN], BF16, kind="ExternalInput").ap()
        tb["acc_out"] = nc.dram_tensor("acc_out", [128, 8 * OWN], F32, kind="ExternalOutput").ap()
    dbg = {}
    if debug_stage == 0:
        dbg["xn"] = nc.dram_tensor("d_xn", [S, D], F32, kind="ExternalOutput").ap()
        dbg["stats"] = nc.dram_tensor("d_stats", [S, 8], F32, kind="ExternalOutput").ap()
    if debug_stage < 4:
        dbg["zu_fm"] = nc.dram_tensor("d_zu_fm", [D, S], BF16, kind="ExternalOutput").ap()
        dbg["zv_fm"] = nc.dram_tensor("d_zv_fm", [D, S], BF16, kind="ExternalOutput").ap()
        dbg["zu_tm"] = nc.dram_tensor("d_zu_tm", [S, D], BF16, kind="ExternalOutput").ap()
        dbg["xnT"] = nc.dram_tensor("d_xnT", [D, OWN], BF16, kind="ExternalOutput").ap()
        dbg["wfull"] = nc.dram_tensor("d_wfull", [128, 128], F32, kind="ExternalOutput").ap()
        if debug_stage >= 2:
            dbg["qh1"] = nc.dram_tensor("d_qh1", [128, 8 * OWN], BF16, kind="ExternalOutput").ap()
            dbg["qh2"] = nc.dram_tensor("d_qh2", [128, 8 * OWN], BF16, kind="ExternalOutput").ap()
        if debug_stage >= 3:
            dbg["acc"] = nc.dram_tensor("d_acc", [128, 8 * OWN], F32, kind="ExternalOutput").ap()

    AB_COLS = 55296
    AF_COLS = 7168

    with ExitStack() as ctx:
        en = ctx.enter_context
        s_t = en(nc.semaphore("s_t"))
        s_v = en(nc.semaphore("s_v"))
        s_s = en(nc.semaphore("s_s"))
        s_ld = en(nc.semaphore("s_ld"))
        s_st = en(nc.semaphore("s_st"))
        s_w = en(nc.semaphore("s_w"))
        sch = Sched([s_t, s_v, s_s, s_ld, s_st, s_w])
        sch.own(s_t, "tensor"); sch.own(s_v, "vector"); sch.own(s_s, "scalar")

        acc = en(nc.sbuf_tensor("acc", [128, 8 * OWN], F32))[:, :]
        idf_sb = en(nc.sbuf_tensor("idf_sb", [128, 128], F32))[:, :]
        idb_sb = en(nc.sbuf_tensor("idb_sb", [128, 128], BF16))[:, :]
        sel_sb = en(nc.sbuf_tensor("sel_sb", [8, 8 * 128], BF16))[:, :]
        rb_sb = en(nc.sbuf_tensor("rb_sb", [128, E], F32))[:, :]
        wfull_sb = en(nc.sbuf_tensor("wfull_sb", [128, 128], F32))[:, :]
        wT_sb = en(nc.sbuf_tensor("wT_sb", [8, OWN], BF16))[:, :]
        stats = en(nc.sbuf_tensor("stats", [128, 24], F32))[:, :]
        rtr = en(nc.sbuf_tensor("rtr", [128, 48], F32))[:, :]
        wr_sb = en(nc.sbuf_tensor("wr_sb", [128, 8 * E], BF16))[:, :]

        negbig = en(nc.sbuf_tensor("negbig", [128, E], F32))[:, :]
        rmask = en(nc.sbuf_tensor("rmask", [128, E], mybir.dt.uint32))[:, :]
        ab = Arena(en(nc.sbuf_tensor("arena_b", [128, AB_COLS], BF16))[:, :], AB_COLS)
        af = Arena(en(nc.sbuf_tensor("arena_f", [128, AF_COLS], F32))[:, :], AF_COLS)

        ps = en(nc.psum_tensor("ps", [128, 4096], F32))[:, :]

        sch.emit("sync", lambda e: e.dma_start(out=idf_sb, in_=identf[:, :]), inc=s_ld, inc_k=16)
        sch.emit("sync", lambda e: e.dma_start(out=idb_sb, in_=identb[:, :]), inc=s_ld, inc_k=16)
        sch.emit("sync", lambda e: e.dma_start(out=sel_sb, in_=sel_c[:, :]), inc=s_ld, inc_k=16)
        sch.emit("sync", lambda e: e.dma_start(out=rb_sb, in_=rb_rep[:, :]), inc=s_ld, inc_k=16)
        const_tok = sch.tok(s_ld)
        sch.emit("vector", lambda e: e.memset(negbig, -BIG), inc=s_v)

        ut_sb = ab.take(8192)
        vt_sb = ab.take(8192)
        for dc in range(8):
            sch.emit("gpsimd", lambda e, dc=dc: e.dma_start(
                out=ut_sb[:, dc * 1024:(dc + 1) * 1024], in_=ut[dc * 128:(dc + 1) * 128, :]),
                inc=s_w, inc_k=16)
            sch.emit("gpsimd", lambda e, dc=dc: e.dma_start(
                out=vt_sb[:, dc * 1024:(dc + 1) * 1024], in_=vt[dc * 128:(dc + 1) * 128, :]),
                inc=s_w, inc_k=16)
        sch.emit("gpsimd", lambda e: e.dma_start(
            out=wr_sb, in_=wr.rearrange("(dc p) e -> p dc e", p=128)), inc=s_w, inc_k=16)
        wprep_tok = sch.tok(s_w)
        first_pc = [False]
        for ei_ in range(E):
            for (dst_, src_) in ((w1b, w1), (w2b, w2), (w3b, w3)):
                sch.emit("gpsimd", lambda e, dst_=dst_, src_=src_, ei_=ei_: e.dma_start(
                    out=dst_[ei_], in_=src_[ei_]),
                    inc=s_w, inc_k=16)
        for (dst_, src_) in ((ws1b, ws1), (ws2b, ws2), (ws3b, ws3), (ub, u_in), (vb, v_in)):
            sch.emit("gpsimd", lambda e, dst_=dst_, src_=src_: e.dma_start(
                out=dst_[:, :], in_=src_[:, :]), inc=s_w, inc_k=16)
        precast_tok = sch.tok(s_w)



        if debug_stage == 5:
            sch.emit("sync", lambda e: e.dma_start(
                out=xnT_hb[:, :], in_=tb["xnT_in"][:, :]), inc=s_st, inc_k=16)
            sch.emit("sync", lambda e: e.dma_start(out=wT_sb, in_=tb["wT_in"][:, :]),
                     inc=s_ld, inc_k=16)
            sch.emit("vector", lambda e: e.memset(acc, 0.0), inc=s_v)
            emit_ffn_phase(nc, sch, (s_t, s_v, s_s, s_ld, s_st, s_w), ab, af, ps, dict(
                acc=acc, sel_sb=sel_sb, wT_sb=wT_sb, xnT_hb=xnT_hb, const_tok=const_tok,
                w1b=w1b, w2b=w2b, w3b=w3b, ws1b=ws1b, ws2b=ws2b, ws3b=ws3b,
                precast_tok=precast_tok))
            fence = sch.fence()
            sch.emit("sync", lambda e: e.dma_start(out=tb["acc_out"][:, :], in_=acc),
                     waits=fence, inc=s_st, inc_k=16)
            sch.emit("sync", waits=[sch.tok(s_st), sch.tok(s_ld), sch.tok(s_w)])
            sch.finalize()
            with nc.Block() as block:
                block.tensor(lambda e: [op(e) for op in sch.ops["tensor"]])
                block.vector(lambda e: [op(e) for op in sch.ops["vector"]])
                block.scalar(lambda e: [op(e) for op in sch.ops["scalar"]])
                block.sync(lambda e: [op(e) for op in sch.ops["sync"]])
                block.gpsimd(lambda e: [op(e) for op in sch.ops["gpsimd"]])
            return nc

        xb = [af.take(1024) for _ in range(3)]
        junk = af.take(1024)
        xn = [ab.take(1024), ab.take(1024)]
        xnT_stage = [ab.take(4096) for _ in range(2)]
        stg_fm = [ab.take(4096), ab.take(4096)]
        stg_tm = [ab.take(4096), ab.take(4096)]

        tr_ps = ps[:, 0:512].bitcast(BF16)
        a2_ps = ps[:, 0:1024]
        fm_ps = [ps[:, 1024:1536], ps[:, 1536:2048]]
        tm_ps2 = [ps[:, 2048:3072], ps[:, 3072:4096]]
        rt_ps = ps[:, 512:544]

        xb_free = [() for _ in range(3)]
        xn_free = [() for _ in range(2)]
        stgfm_free = [(), ()]
        stgtm_free = [(), ()]
        stage_free = [(), ()]
        fm_free = [(), ()]
        tm_free2 = [(), ()]
        tm_i = 0
        tr_free = ()
        rt_free = ()
        fm_i = 0
        stgfm_i = 0
        junk_free = [()]
        tr_free = [()]
        INV_D = 1.0 / D

        tile_info = {}
        pending = None

        def emit_tile_front(j):
            sl = j % 3
            c0 = sl * 8
            ldt = sch.emit("sync", lambda e, j=j, sl=sl: e.dma_start(
                out=xb[sl], in_=xf[j * 128:(j + 1) * 128, :]),
                waits=xb_free[sl], inc=s_ld, inc_k=16)
            ms_tok = sch.emit("vector", lambda e, c0=c0: e.memset(
                stats[:, c0:c0 + 2], 0.0), inc=s_v)
            sm_tok = sch.emit("scalar", lambda e, sl=sl, c0=c0: e.activation(
                junk, xb[sl], AF.Identity, accum_out=stats[:, c0:c0 + 1]),
                waits=[ldt, ms_tok], hard_waits=list(junk_free[0]), inc=s_s)
            sq_tok = sch.emit("scalar", lambda e, sl=sl, c0=c0: e.activation(
                junk, xb[sl], AF.Square, accum_out=stats[:, c0 + 1:c0 + 2]),
                hard_waits=[sm_tok], inc=s_s)
            junk_free[0] = [sq_tok]
            sch.emit("vector", lambda e, c0=c0: e.tensor_scalar_mul(
                stats[:, c0 + 2:c0 + 3], stats[:, c0:c0 + 1], INV_D),
                waits=[sq_tok])
            t5 = sch.emit("vector", lambda e, c0=c0: e.tensor_scalar_mul(
                stats[:, c0 + 3:c0 + 4], stats[:, c0 + 1:c0 + 2], INV_D), inc=s_v)
            t6 = sch.emit("vector", lambda e, c0=c0: e.tensor_mul(
                stats[:, c0 + 4:c0 + 5], stats[:, c0 + 2:c0 + 3], stats[:, c0 + 2:c0 + 3]),
                hard_waits=[t5], inc=s_v)
            t7 = sch.emit("vector", lambda e, c0=c0: e.tensor_sub(
                stats[:, c0 + 4:c0 + 5], stats[:, c0 + 3:c0 + 4], stats[:, c0 + 4:c0 + 5]),
                hard_waits=[t6], inc=s_v)
            t8 = sch.emit("vector", lambda e, c0=c0: e.tensor_scalar(
                stats[:, c0 + 4:c0 + 5], stats[:, c0 + 4:c0 + 5], EPS, None, op0=ALU.add),
                hard_waits=[t7], inc=s_v)
            rec_tok = sch.emit("vector", lambda e, c0=c0: e.reciprocal(
                stats[:, c0 + 5:c0 + 6], stats[:, c0 + 4:c0 + 5]),
                hard_waits=[t8], inc=s_v)
            tile_info[j] = dict(sl=sl, c0=c0, rec=rec_tok)

        def emit_sqrt_nm(j):
            ti = tile_info[j]
            c0 = ti["c0"]
            sst_tok = sch.emit("scalar", lambda e, c0=c0: e.activation(
                stats[:, c0 + 6:c0 + 7], stats[:, c0 + 5:c0 + 6], AF.Sqrt),
                waits=[ti["rec"]], inc=s_s)
            nm_tok = sch.emit("vector", lambda e, c0=c0: e.scalar_tensor_tensor(
                stats[:, c0 + 7:c0 + 8], stats[:, c0 + 2:c0 + 3], -1.0,
                stats[:, c0 + 6:c0 + 7], op0=ALU.mult, op1=ALU.mult),
                waits=[sst_tok], inc=s_v)
            ti["nm"] = nm_tok

        def emit_tile_back(j):
            ti = tile_info[j]
            sl, c0 = ti["sl"], ti["c0"]
            tt = j % 4
            st_sl = (j // 4) % 2
            xsl = j % 2
            xn_tok = sch.emit("scalar", lambda e, sl=sl, c0=c0, xsl=xsl: e.activation(
                xn[xsl], xb[sl], AF.Identity,
                scale=stats[:, c0 + 6:c0 + 7], bias=stats[:, c0 + 7:c0 + 8]),
                waits=[ti["nm"]] + list(xn_free[xsl]), inc=s_s)
            xb_free[sl] = [xn_tok]

            def do_tr(e, xsl=xsl):
                r = None
                for dc in range(8):
                    r = e.transpose(tr_ps[:, dc * 128:(dc + 1) * 128],
                                    xn[xsl][:, dc * 128:(dc + 1) * 128], idb_sb)
                return r
            tr_tok = sch.emit("tensor", do_tr,
                              waits=[xn_tok, const_tok] + list(tr_free[0]),
                              inc=s_t)
            xn_free[xsl] = [tr_tok]
            ev_tok = sch.emit("vector", lambda e, st_sl=st_sl, tt=tt: e.tensor_copy(
                xnT_stage[st_sl].rearrange("p (dc t) -> p dc t", dc=8)[:, :, tt * 128:(tt + 1) * 128],
                tr_ps.rearrange("p (dc t) -> p dc t", dc=8)),
                waits=[tr_tok] + list(stage_free[st_sl] if tt == 0 else ()), inc=s_v)
            tr_free[0] = [ev_tok]
            ti["ev"] = ev_tok

        def emit_group_projs(g):
            nonlocal fm_i, tm_i, rt_free, sp_tok_last
            st_sl = g % 2
            stage_rdy = [tile_info[g * 4 + 3]["ev"]]
            for (wsb, dst, fsl) in (((ut_sb, zu_fm, 0), (vt_sb, zv_fm, 1)) if debug_stage >= 1 else ()):
                ev_toks_fm = []
                for fc in range(8):
                    pi = fm_i % 2

                    def do_fm(e, wsb=wsb, fc=fc, st_sl=st_sl, pi=pi):
                        r = None
                        for dc in range(8):
                            r = e.matmul(fm_ps[pi][:, 0:512],
                                         wsb[:, dc * 1024 + fc * 128: dc * 1024 + (fc + 1) * 128],
                                         xnT_stage[st_sl][:, dc * 512:(dc + 1) * 512],
                                         start=(dc == 0), stop=(dc == 7))
                        return r
                    mm_tok = sch.emit("tensor", do_fm,
                                      waits=stage_rdy + [wprep_tok] + list(fm_free[pi]), inc=s_t)
                    ev = sch.emit("vector", lambda e, fc=fc, pi=pi, fsl=fsl: e.tensor_scalar_mul(
                        stg_fm[fsl][:, fc * 512:(fc + 1) * 512], fm_ps[pi][:, 0:512], SCALE),
                        waits=[mm_tok] + (list(stgfm_free[fsl]) if fc == 0 else []), inc=s_v)
                    fm_free[pi] = [ev]
                    ev_toks_fm.append(ev)
                    fm_i += 1
                stok = sch.emit("sync", lambda e, dst=dst, g=g, fsl=fsl: e.dma_start(
                    out=dst[:, g * 512:(g + 1) * 512].rearrange("(fc p) t -> p fc t", p=128),
                    in_=stg_fm[fsl].rearrange("p (fc t) -> p fc t", fc=8)),
                    waits=[ev_toks_fm[-1]], inc=s_st, inc_k=16)
                stgfm_free[fsl] = [stok]

            for (wsb, dst, sti) in (((ut_sb, zu_tm, 0), (vt_sb, zv_tm, 1)) if debug_stage >= 1 else ()):
                ev_toks = []
                for tt in range(4):
                    tsl = tm_i % 2

                    def do_tm(e, wsb=wsb, st_sl=st_sl, tt=tt, tsl=tsl):
                        r = None
                        for fh in range(2):
                            for dc in range(8):
                                r = e.matmul(tm_ps2[tsl][:, fh * 512:(fh + 1) * 512],
                                             xnT_stage[st_sl][:, dc * 512 + tt * 128: dc * 512 + (tt + 1) * 128],
                                             wsb[:, dc * 1024 + fh * 512: dc * 1024 + (fh + 1) * 512],
                                             start=(dc == 0), stop=(dc == 7))
                        return r
                    mm_tok = sch.emit("tensor", do_tm,
                                      waits=stage_rdy + [wprep_tok] + list(tm_free2[tsl]) + list(stgtm_free[sti] if tt == 0 else ()),
                                      inc=s_t)
                    ev = sch.emit("vector", lambda e, sti=sti, tt=tt, tsl=tsl: e.tensor_scalar_mul(
                        stg_tm[sti][:, tt * 1024:(tt + 1) * 1024], tm_ps2[tsl], SCALE),
                        waits=[mm_tok], inc=s_v)
                    tm_free2[tsl] = [ev]
                    tm_i += 1
                    ev_toks.append(ev)
                stok = sch.emit("sync", lambda e, sti=sti, dst=dst, g=g: e.dma_start(
                    out=dst[g * 512:(g + 1) * 512, :].rearrange("(tt p) f -> p tt f", p=128),
                    in_=stg_tm[sti].rearrange("p (tt f) -> p tt f", tt=4)),
                    waits=ev_toks, inc=s_st, inc_k=16)
                stgtm_free[sti] = [stok]

            if g < OWN_G:
                sp_tok = sch.emit("sync", lambda e, st_sl=st_sl, g=g: e.dma_start(
                    out=xnT_hb[:, g * 512:(g + 1) * 512].rearrange("(dc p) t -> p dc t", p=128),
                    in_=xnT_stage[st_sl].rearrange("p (dc t) -> p dc t", dc=8)),
                    waits=stage_rdy, inc=s_st, inc_k=16)
                for tt in (range(4) if debug_stage >= 1 else ()):
                    j = g * 4 + tt

                    def do_rt(e, st_sl=st_sl, tt=tt):
                        r = None
                        for dc in range(8):
                            r = e.matmul(rt_ps[:, tt * 8:(tt + 1) * 8],
                                         xnT_stage[st_sl][:, dc * 512 + tt * 128: dc * 512 + (tt + 1) * 128],
                                         wr_sb[:, dc * 8:(dc + 1) * 8],
                                         start=(dc == 0), stop=(dc == 7))
                        return r
                    rt_tok = sch.emit("tensor", do_rt,
                                      waits=stage_rdy + [wprep_tok] + list(rt_free), inc=s_t)
                    r1 = sch.emit("vector", lambda e, tt=tt: e.tensor_scalar_mul(
                        rtr[:, 0:8], rt_ps[:, tt * 8:(tt + 1) * 8], SCALE),
                        waits=[rt_tok, const_tok], inc=s_v)
                    rt_free = [r1]
                    r2 = sch.emit("vector", lambda e: e.tensor_add(rtr[:, 8:16], rtr[:, 0:8], rb_sb),
                                  hard_waits=[r1], inc=s_v)
                    r3 = sch.emit("vector", lambda e: e.reduce_max(rtr[:, 40:41], rtr[:, 8:16], axis=AX),
                                  hard_waits=[r2], inc=s_v)
                    r4 = sch.emit("vector", lambda e: e.tensor_scalar(
                        rmask, rtr[:, 8:16], rtr[:, 40:41], None, op0=ALU.is_lt),
                        hard_waits=[r3], inc=s_v)
                    r5 = sch.emit("vector", lambda e: e.select(
                        rtr[:, 16:24], rmask, rtr[:, 8:16], negbig, add_drain=True),
                        hard_waits=[r4], inc=s_v)
                    r7 = sch.emit("vector", lambda e: e.reduce_max(rtr[:, 41:42], rtr[:, 16:24], axis=AX),
                                  hard_waits=[r5], inc=s_v)
                    r8 = sch.emit("vector", lambda e: e.tensor_scalar(
                        rmask, rtr[:, 8:16], rtr[:, 41:42], None, op0=ALU.is_lt),
                        hard_waits=[r7], inc=s_v)
                    r9 = sch.emit("vector", lambda e: e.select(
                        rtr[:, 24:32], rmask, negbig, rtr[:, 0:8], add_drain=True),
                        hard_waits=[r8], inc=s_v)
                    r10 = sch.emit("vector", lambda e: e.reduce_max(rtr[:, 42:43], rtr[:, 24:32], axis=AX),
                                   hard_waits=[r9], inc=s_v)
                    zt_tok = sch.emit("vector", lambda e: e.tensor_scalar_mul(
                        rtr[:, 43:44], rtr[:, 42:43], -1.0), hard_waits=[r10], inc=s_v)
                    ex_tok = sch.emit("scalar", lambda e: e.activation(
                        rtr[:, 32:40], rtr[:, 24:32], AF.Exp, bias=rtr[:, 43:44]),
                        waits=[zt_tok], inc=s_s)
                    r12 = sch.emit("vector", lambda e: e.reduce_sum(rtr[:, 44:45], rtr[:, 32:40], axis=AX),
                                   waits=[ex_tok], inc=s_v)
                    r13 = sch.emit("vector", lambda e: e.reciprocal(rtr[:, 45:46], rtr[:, 44:45]),
                                   hard_waits=[r12], inc=s_v)
                    sch.emit("vector", lambda e, j=j: e.tensor_scalar_mul(
                        wfull_sb[:, j * 8:(j + 1) * 8], rtr[:, 32:40], rtr[:, 45:46]),
                        hard_waits=[r13], inc=s_v)

            stage_free[st_sl] = [sch.tok(s_t)] + ([sp_tok] if g < OWN_G else [])

        sp_tok_last = None
        for j in range(A_NG * 4):
            emit_tile_front(j)
            if pending is not None:
                emit_tile_back(pending)
                if pending % 4 == 3:
                    emit_group_projs(pending // 4)
            pending = j
            emit_sqrt_nm(j)
        emit_tile_back(pending)
        emit_group_projs(A_NG - 1)

        wf_tok = sch.tok(s_v)
        for j in range(OWN_T):
            t_tok = sch.emit("tensor", lambda e, j=j: e.transpose(
                a2_ps[0:8, 0:128], wfull_sb[:, j * 8:(j + 1) * 8], idf_sb),
                waits=[wf_tok] + list(tr_free[0]), inc=s_t)
            ev = sch.emit("vector", lambda e, j=j: e.tensor_copy(
                wT_sb[0:8, j * 128:(j + 1) * 128], a2_ps[0:8, 0:128]),
                waits=[t_tok], inc=s_v)
            tr_free[0] = [ev]

        qh1_sb = qh2_sb = None
        if debug_stage >= 2:
            fence_a = sch.fence()
            ab.reset_to(0)
            af.reset_to(0)
            qh1_sb = ab.take(8 * OWN)
            qh2_sb = ab.take(8 * OWN)
            fmbuf = ab.take(S)
            ownbuf = ab.take(OWN)
            tmbuf = [ab.take(4096), ab.take(4096)]
            A_sb2 = [[ab.take(1024), ab.take(1024)], [ab.take(1024), ab.take(1024)]]
            STRIP_END = ab.off

            psc = [ps[:, 0:1024], ps[:, 1024:2048]]
            qh_ps = ps[:, 2048:4096]

            slot_free = [list(fence_a), list(fence_a)]
            tmb_free = [list(fence_a), list(fence_a)]
            last_sig = {0: list(fence_a), 1: list(fence_a)}
            last_acc = {(p, b): () for p in range(2) for b in range(2)}
            qh_evac = list(fence_a)

            def strip(fm_src, tm_src, own_src, qh_out):
                nonlocal qh_evac
                sig_tok = {}
                acc_tok = {}
                for c in range(STRIP_C):
                    sl = c % 2
                    r0 = 64 * sl
                    fm_t = sch.emit("sync", lambda e, c=c, r0=r0, fm_src=fm_src: e.dma_start(
                        out=fmbuf[r0:r0 + 64, :], in_=fm_src[c * 64:(c + 1) * 64, :]),
                        waits=slot_free[sl], inc=s_ld, inc_k=16)
                    own_t = sch.emit("sync", lambda e, c=c, r0=r0, own_src=own_src: e.dma_start(
                        out=ownbuf[r0:r0 + 64, :], in_=own_src[c * 64:(c + 1) * 64, 0:OWN]),
                        waits=slot_free[sl], inc=s_ld, inc_k=16)
                    tm_t = {}
                    for jj in range(8):
                        tm_t[jj] = sch.emit("sync", lambda e, c=c, jj=jj, sl=sl, tm_src=tm_src: e.dma_start(
                            out=tmbuf[sl][:, jj * 512:(jj + 1) * 512],
                            in_=tm_src[jj * 1024:(jj + 1) * 1024, c * 64:(c + 1) * 64]
                            .rearrange("(j p) f -> p j f", p=128)),
                            waits=tmb_free[sl], inc=s_ld, inc_k=16)
                    loads = [fm_t, own_t]

                    def emit_acc(t, only_b, c=c, sl=sl):
                        tp2 = t % 2
                        for b in (only_b,):
                            def do_acc(e, c=c, sl=sl, t=t, b=b, tp2=tp2):
                                r = None
                                p0 = 64 * (c % 2)
                                for sg in range(2):
                                    sgg = b * 2 + sg
                                    r = e.matmul(qh_ps[p0:p0 + 64, sgg * 512:(sgg + 1) * 512],
                                                 tmbuf[sl][:, t * 64:(t + 1) * 64],
                                                 A_sb2[tp2][b][:, sg * 512:(sg + 1) * 512],
                                                 start=(t == 0), stop=(t == STRIP_T - 1),
                                                 tile_position=(0, p0))
                                return r
                            aw = [sig_tok[(t, b)], tm_t[t // 8]]
                            if t == 0 and sl == 0:
                                aw = aw + qh_evac
                            acc_tok[(t, b)] = sch.emit("tensor", do_acc, waits=aw, inc=s_t)
                            last_acc[(tp2, b)] = [acc_tok[(t, b)]]

                    for t in range(STRIP_T):
                        tp2 = t % 2
                        for b in range(2):
                            def do_sc(e, r0=r0, t=t, b=b):
                                r = None
                                for sg in range(2):
                                    sgg = b * 2 + sg
                                    r = e.matmul(psc[b][:, sg * 512:(sg + 1) * 512],
                                                 fmbuf[r0:r0 + 64, t * 128:(t + 1) * 128],
                                                 ownbuf[r0:r0 + 64, sgg * 512:(sgg + 1) * 512],
                                                 start=True, stop=True)
                                return r
                            sc_tok = sch.emit("tensor", do_sc,
                                              waits=loads + list(last_sig[b]), inc=s_t)
                            sig_tok[(t, b)] = sch.emit("scalar", lambda e, b=b, tp2=tp2: e.activation(
                                A_sb2[tp2][b], psc[b], AF.Sigmoid),
                                waits=[sc_tok] + list(last_acc[(tp2, b)]), inc=s_s)
                            last_sig[b] = [sig_tok[(t, b)]]
                            if t > 0:
                                emit_acc(t - 1, only_b=b)
                    emit_acc(STRIP_T - 1, only_b=0)
                    emit_acc(STRIP_T - 1, only_b=1)
                    slot_free[sl] = [acc_tok[(STRIP_T - 1, 1)]]
                    tmb_free[sl] = [acc_tok[(STRIP_T - 1, 1)]]
                    if sl == 1:
                        ev = sch.emit("vector", lambda e, c=c, qh_out=qh_out: e.tensor_scalar_mul(
                            qh_out[:, (c // 2) * OWN:(c // 2 + 1) * OWN], qh_ps, SCALE),
                            waits=[acc_tok[(STRIP_T - 1, 0)], acc_tok[(STRIP_T - 1, 1)]], inc=s_v)
                        qh_evac = [ev]

            strip(zv_fm, zv_tm, zu_fm, qh1_sb)
            strip(zu_fm, zu_tm, zv_fm, qh2_sb)

        if debug_stage >= 3:
            fence_s = sch.fence()
            ab.reset_to(32768)
            u_sb = ab.take(8192)
            v_sb = ab.take(8192)
            for fc in range(8):
                sch.emit("sync", lambda e, fc=fc: e.dma_start(
                    out=u_sb[:, fc * 1024:(fc + 1) * 1024], in_=ub[fc * 128:(fc + 1) * 128, :]),
                    waits=fence_s + [precast_tok], inc=s_ld, inc_k=16)
                sch.emit("sync", lambda e, fc=fc: e.dma_start(
                    out=v_sb[:, fc * 1024:(fc + 1) * 1024], in_=vb[fc * 128:(fc + 1) * 128, :]),
                    waits=fence_s + [precast_tok], inc=s_ld, inc_k=16)
            uv_tok = sch.tok(s_ld)

            g_ps = [ps[:, 0:512], ps[:, 512:1024]]
            g_free = [list(fence_s), list(fence_s)]
            gi = 0
            for dc in range(8):
                for sg in range(4):
                    pi = gi % 2

                    def do_g(e, dc=dc, sg=sg, pi=pi):
                        r = None
                        for fc in range(8):
                            e.matmul(g_ps[pi], u_sb[:, fc * 1024 + dc * 128: fc * 1024 + (dc + 1) * 128],
                                     qh1_sb[:, fc * OWN + sg * 512: fc * OWN + (sg + 1) * 512],
                                     start=(fc == 0), stop=False)
                        for fc in range(8):
                            r = e.matmul(g_ps[pi], v_sb[:, fc * 1024 + dc * 128: fc * 1024 + (dc + 1) * 128],
                                         qh2_sb[:, fc * OWN + sg * 512: fc * OWN + (sg + 1) * 512],
                                         start=False, stop=(fc == 7))
                        return r
                    mm = sch.emit("tensor", do_g, waits=[uv_tok] + fence_s + g_free[pi], inc=s_t)
                    ev = sch.emit("vector", lambda e, dc=dc, sg=sg, pi=pi: e.tensor_copy(
                        acc[:, dc * OWN + sg * 512: dc * OWN + (sg + 1) * 512], g_ps[pi]),
                        waits=[mm], inc=s_v)
                    g_free[pi] = [ev]
                    gi += 1

        if debug_stage >= 4:
            emit_ffn_phase(nc, sch, (s_t, s_v, s_s, s_ld, s_st, s_w), ab, af, ps, dict(
                acc=acc, sel_sb=sel_sb, wT_sb=wT_sb, xnT_hb=xnT_hb, const_tok=const_tok,
                w1b=w1b, w2b=w2b, w3b=w3b, ws1b=ws1b, ws2b=ws2b, ws3b=ws3b,
                precast_tok=precast_tok))

            fence_f = sch.fence()
            if not RESID:
                fence_f = fence_f
            af.reset_to(2048)
            xb2 = [af.take(1024), af.take(1024)]
            ob = [af.take(1024), af.take(1024)]
            tr_free2 = list(fence_f)
            xb2_free = [list(fence_f), list(fence_f)]
            ob_free = [list(fence_f), list(fence_f)]
            for j in (range(OWN_T) if RESID else ()):
                sl = j % 2
                x_tok = sch.emit("sync", lambda e, j=j, sl=sl: e.dma_start(
                    out=xb2[sl], in_=xf[j * 128:(j + 1) * 128, :]),
                    waits=xb2_free[sl], inc=s_ld, inc_k=16)

                def do_trr(e, j=j):
                    r = None
                    for dc in range(8):
                        r = e.transpose(ps[:, dc * 128:(dc + 1) * 128],
                                        acc[:, dc * OWN + j * 128: dc * OWN + (j + 1) * 128],
                                        idf_sb)
                    return r
                tr_tok = sch.emit("tensor", do_trr, waits=fence_f + tr_free2, inc=s_t)
                o_tok = sch.emit("vector", lambda e, sl=sl: e.tensor_add(
                    ob[sl], ps[:, 0:1024], xb2[sl]),
                    waits=[tr_tok, x_tok] + ob_free[sl], inc=s_v)
                tr_free2 = [o_tok]
                xb2_free[sl] = [o_tok]
                st_tok = sch.emit("sync", lambda e, j=j, sl=sl: e.dma_start(
                    out=out[j * 128:(j + 1) * 128, :], in_=ob[sl]),
                    waits=[o_tok], inc=s_st, inc_k=16)
                ob_free[sl] = [st_tok]

        if debug_stage < 4:
            fence = sch.fence()
            pairs = [("zu_fm", zu_fm), ("zv_fm", zv_fm), ("zu_tm", zu_tm), ("xnT", xnT_hb)]
            for (k, src) in pairs:
                sch.emit("sync", lambda e, k=k, src=src: e.dma_start(out=dbg[k][:, :], in_=src[:, :]),
                         waits=fence, inc=s_st, inc_k=16)
            sch.emit("sync", lambda e: e.dma_start(out=dbg["wfull"][:, :], in_=wfull_sb),
                     waits=fence, inc=s_st, inc_k=16)
            if debug_stage >= 2:
                sch.emit("sync", lambda e: e.dma_start(out=dbg["qh1"][:, :], in_=qh1_sb),
                         waits=fence, inc=s_st, inc_k=16)
                sch.emit("sync", lambda e: e.dma_start(out=dbg["qh2"][:, :], in_=qh2_sb),
                         waits=fence, inc=s_st, inc_k=16)
            if debug_stage >= 3:
                sch.emit("sync", lambda e: e.dma_start(out=dbg["acc"][:, :], in_=acc),
                         waits=fence, inc=s_st, inc_k=16)

        sch.emit("sync", waits=[sch.tok(s_st), sch.tok(s_ld), sch.tok(s_w)])
        sch.finalize()

        with nc.Block() as block:
            block.tensor(lambda e: [op(e) for op in sch.ops["tensor"]])
            block.vector(lambda e: [op(e) for op in sch.ops["vector"]])
            block.scalar(lambda e: [op(e) for op in sch.ops["scalar"]])
            block.sync(lambda e: [op(e) for op in sch.ops["sync"]])
            block.gpsimd(lambda e: [op(e) for op in sch.ops["gpsimd"]])

    return nc


def kernel(L0_attn_qz, L0_attn_ternary_factor_u, L0_attn_ternary_factor_v,
           L0_ffn_Wr, L0_ffn_rb, L0_ffn_W1, L0_ffn_W2, L0_ffn_W3,
           L0_ffn_Ws1, L0_ffn_Ws2, L0_ffn_Ws3):
    x = np.ascontiguousarray(np.asarray(L0_attn_qz, dtype=np.float32))
    u = np.ascontiguousarray(np.asarray(L0_attn_ternary_factor_u, dtype=np.float32))
    v = np.ascontiguousarray(np.asarray(L0_attn_ternary_factor_v, dtype=np.float32))
    wr_np = np.ascontiguousarray(np.asarray(L0_ffn_Wr, dtype=np.float32))
    rb_np = np.asarray(L0_ffn_rb, dtype=np.float32)
    w1_np = np.ascontiguousarray(np.asarray(L0_ffn_W1, dtype=np.float32))
    w2_np = np.ascontiguousarray(np.asarray(L0_ffn_W2, dtype=np.float32))
    w3_np = np.ascontiguousarray(np.asarray(L0_ffn_W3, dtype=np.float32))
    ws1_np = np.ascontiguousarray(np.asarray(L0_ffn_Ws1, dtype=np.float32))
    ws2_np = np.ascontiguousarray(np.asarray(L0_ffn_Ws2, dtype=np.float32))
    ws3_np = np.ascontiguousarray(np.asarray(L0_ffn_Ws3, dtype=np.float32))

    ut_np = np.ascontiguousarray(u.T)
    vt_np = np.ascontiguousarray(v.T)
    rb_rep = np.ascontiguousarray(np.broadcast_to(rb_np[None, :], (128, E)))

    nc = build_nc(DEBUG_STAGE)

    in_maps = []
    for k in range(8):
        b, q = k // 4, k % 4
        idx = np.r_[q * OWN:(q + 1) * OWN, 0:q * OWN, (q + 1) * OWN:S]
        xfk = np.ascontiguousarray(x[b][idx])
        in_maps.append({
            "xf": xfk, "ut": ut_np, "vt": vt_np, "u_in": u, "v_in": v,
            "wr": wr_np, "rb_rep": rb_rep,
            "w1": w1_np, "w2": w2_np, "w3": w3_np,
            "ws1": ws1_np, "ws2": ws2_np, "ws3": ws3_np,
        })

    res = run_bass_kernel_spmd(nc, in_maps, core_ids=list(range(8)))
    kernel._last_results = res

    outp = np.empty((2, S, D), dtype=np.float32)
    for k in range(8):
        b, q = k // 4, k % 4
        outp[b, q * OWN:(q + 1) * OWN] = res.results[k]["out"]
    return outp
